# version 01
# baseline (speedup 1.0000x reference)
import numpy as np
import ml_dtypes
from contextlib import ExitStack

import concourse.bass as bass
import concourse.mybir as mybir
from concourse.bass_utils import run_bass_kernel_spmd

F32 = mybir.dt.float32
BF16 = mybir.dt.bfloat16
I32 = mybir.dt.int32
AF = mybir.ActivationFunctionType
OP = mybir.AluOpType
AX = mybir.AxisListType
BF = ml_dtypes.bfloat16

NCORES = 8
N = 16384
T = N // NCORES
D = 1024
NT = D // 128
CH = 512
NCH = T // CH
HEADS = 8
DH = 64
HID = 2730
HIDP = 2816
NHP = HIDP // 128
W1W = 2 * HIDP
XW = T + 4
NEG = -1.0e30

FCONS = (0.0, 1.0, -1.0, -0.5, 1e-5, 1e-6, 1e-24)

ROLES = ("pe", "dve", "act", "pool", "sp")
_R2A = {"pe": "tensor", "dve": "vector", "act": "scalar", "pool": "gpsimd", "sp": "sync"}



def _rotary_tables():
    half = np.arange(0, DH, 2, dtype=np.float32)
    inv_freq = (1.0 / 10000.0 ** (half / DH)).astype(np.float32)
    t = np.arange(N, dtype=np.float32)
    fr = t[:, None] * inv_freq[None, :]
    fr = np.concatenate([fr, fr], axis=-1)
    base = ((half + 0.4 * DH) / (1.4 * DH)).astype(np.float32)
    power = ((t - N // 2) / 256.0).astype(np.float32)
    sc = base[None, :] ** power[:, None]
    sc = np.concatenate([sc, sc], axis=-1)
    cos = np.cos(fr).astype(np.float32)
    sin = np.sin(fr).astype(np.float32)
    sign = np.where(np.arange(64) < 32, -1.0, 1.0).astype(np.float32)
    QK = 8.0
    qcs = (cos * sc * QK).astype(np.float32)
    qss = (sign[None, :] * sin * sc * QK).astype(np.float32)
    kcs = (cos / sc).astype(np.float32)
    kss = (sign[None, :] * sin / sc).astype(np.float32)
    def fm(a):
        a = np.ascontiguousarray(a.T)
        return np.concatenate([a, a], axis=0)
    return fm(qcs), fm(qss), fm(kcs), fm(kss)


def _col_fm(vec):
    v = np.asarray(vec, np.float32).reshape(NT, 128).T
    return np.ascontiguousarray(v)


def _host_prep(ins):
    x = np.asarray(ins["L0_attn_x"], np.float32)[0]
    xT = np.ascontiguousarray(x.T)

    wqkv_a = np.ascontiguousarray(np.asarray(ins["L0_attn_W_qkv"], np.float32).T)
    wout_t = np.ascontiguousarray(np.asarray(ins["L0_attn_W_out"], np.float32).T)
    bout_c = _col_fm(ins["L0_attn_b_out"])

    fwqkv = np.ascontiguousarray(np.asarray(ins["L0_ffn_wqkv"], np.float32))
    fwo = np.ascontiguousarray(np.asarray(ins["L0_ffn_wo"], np.float32))
    w1 = np.asarray(ins["L0_ffn_w1"], np.float32)
    b1 = np.asarray(ins["L0_ffn_b1"], np.float32)
    w2 = np.asarray(ins["L0_ffn_w2"], np.float32)
    w1p = np.zeros((2, D, W1W), np.float32)
    b1p = np.zeros((2, W1W), np.float32)
    a_part = np.zeros((2, D, HIDP), np.float32); a_part[:, :, :HID] = w1[:, :, :HID]
    g_part = np.zeros((2, D, HIDP), np.float32); g_part[:, :, :HID] = w1[:, :, HID:]
    ab = np.zeros((2, HIDP), np.float32); ab[:, :HID] = b1[:, :HID]
    gb = np.zeros((2, HIDP), np.float32); gb[:, :HID] = b1[:, HID:]
    for hp in range(NHP):
        w1p[:, :, 256 * hp:256 * hp + 128] = a_part[:, :, 128 * hp:128 * hp + 128]
        w1p[:, :, 256 * hp + 128:256 * hp + 256] = g_part[:, :, 128 * hp:128 * hp + 128]
        b1p[:, 256 * hp:256 * hp + 128] = ab[:, 128 * hp:128 * hp + 128]
        b1p[:, 256 * hp + 128:256 * hp + 256] = gb[:, 128 * hp:128 * hp + 128]
    w2p = np.zeros((2, HIDP, D), np.float32)
    w2p[:, :HID, :] = w2
    b1c = np.zeros((2, 128, 2 * NHP), np.float32)
    for hp in range(NHP):
        b1c[:, :, 2 * hp] = b1p[:, 256 * hp:256 * hp + 128]
        b1c[:, :, 2 * hp + 1] = b1p[:, 256 * hp + 128:256 * hp + 256]
    b2c = np.stack([_col_fm(np.asarray(ins["L0_ffn_b2"], np.float32)[l]) for l in range(2)])
    lngc = np.stack([_col_fm(np.asarray(ins["L0_ffn_ln_g"], np.float32)[l]) for l in range(2)])
    lnbc = np.stack([_col_fm(np.asarray(ins["L0_ffn_ln_b"], np.float32)[l]) for l in range(2)])
    qs = np.asarray(ins["L0_ffn_q_scale"], np.float32)
    ks = np.asarray(ins["L0_ffn_k_scale"], np.float32)
    qsc = np.concatenate([qs, qs], axis=1).T.copy()
    ksc = np.concatenate([ks, ks], axis=1).T.copy()
    qsw = np.concatenate([qs[:, 32:], qs[:, :32]], axis=1)
    ksw = np.concatenate([ks[:, 32:], ks[:, :32]], axis=1)
    qsc_sw = np.concatenate([qsw, qsw], axis=1).T.copy()
    ksc_sw = np.concatenate([ksw, ksw], axis=1).T.copy()

    qcs, qss, kcs, kss = _rotary_tables()

    ident = np.eye(128, dtype=BF)
    ones128b = np.ones((128, 1), dtype=BF)
    onesf = np.ones((1, 128), np.float32)
    qones = np.zeros((128, 32), dtype=BF)
    for t in range(4):
        for h in (2 * t, 2 * t + 1):
            qones[64 * (h % 2):64 * (h % 2) + 64, 8 * t + h] = 1
    qsel = np.zeros((8, 512), dtype=BF)
    for t in range(4):
        for m in range(128):
            qsel[2 * t + (m >= 64), 128 * t + m] = 1
    iv = np.arange(128)[:, None]; jv = np.arange(128)[None, :]
    tri_lo = np.where(jv < iv, np.float32(NEG), np.float32(0)).astype(np.float32)
    tri_hi = np.where(jv > iv, np.float32(NEG), np.float32(0)).astype(np.float32)

    per_core = []
    shared = dict(
        wqkv_a=wqkv_a, wout_t=wout_t, bout_c=bout_c,
        fwqkv=fwqkv, fwo=fwo, fw1=w1p, fb1=b1c, fw2=w2p, fb2=b2c,
        flng=lngc, flnb=lnbc, fqs=qsc, fks=ksc, fqs_sw=qsc_sw, fks_sw=ksc_sw,
        ident=ident, ones128b=ones128b, onesf=onesf, qones=qones, qsel=qsel,
        tri_lo=tri_lo, tri_hi=tri_hi,
    )
    widths = [512, 384, 256, 128]
    offs = np.cumsum([0] + widths)
    for c in range(NCORES):
        s = c * T
        mlo = np.zeros((128, 1280), np.float32)
        for j in range(4):
            w = widths[j]
            blk = mlo[:, offs[j]:offs[j] + w]
            if c == 0:
                blk[:] = NEG
            else:
                blk[:, :128] = tri_lo
        m = dict(shared)
        m.update(
            xt=np.ascontiguousarray(xT[:, s:s + T]),
            qcs=np.ascontiguousarray(qcs[:, s:s + T]),
            qss=np.ascontiguousarray(qss[:, s:s + T]),
            kcs=np.ascontiguousarray(kcs[:, s:s + T]),
            kss=np.ascontiguousarray(kss[:, s:s + T]),
            mlo=mlo,
            gate=np.full((128, 1), 0.0 if c == 0 else 1.0, np.float32),
            rkprev=np.full((1, 1), max(c - 1, 0), np.int32),
            rkself=np.full((1, 1), c, np.int32),
            fcon=np.tile(np.asarray([FCONS], np.float32), (128, 1)),
        )
        per_core.append(m)
    return per_core



class Cx:

    def __init__(self, b, role, eng, counts):
        self.b = b
        self.role = role
        self.eng = eng
        self.counts = counts
        self.pe = role == "pe"
        self.dve = role == "dve"
        self.act = role == "act"
        self.pool = role == "pool"
        self.sp = role == "sp"
        self.t = eng if self.pe else None
        self.v = eng if self.dve else None
        self.s = eng if self.act else None
        self.g = eng if self.pool else None
        self.q = eng if self.sp else None

    def inc(self, ch, instr, k=1):
        self.counts[ch] = self.counts.get(ch, 0) + k
        if instr is not None:
            instr.then_inc(self.b.sems[ch], k)
        return self.counts[ch]

    def bump(self, ch, k=1):
        self.counts[ch] = self.counts.get(ch, 0) + k
        return self.counts[ch]

    def wcur(self, role_flag, ch):
        if role_flag:
            self.eng.wait_ge(self.b.sems[ch], self.counts.get(ch, 0))

    def wabs(self, role_flag, ch, val):
        if role_flag:
            self.eng.wait_ge(self.b.sems[ch], val)


class Build:
    SERIAL = False

    def sync_pt(self, cx):
        if not Build.SERIAL:
            return
        for ch in ("pe", "dve", "act", "pool", "evA", "stD", "dw", "ds", "dg"):
            if cx.counts.get(ch):
                cx.eng.wait_ge(self.sems[ch], cx.counts[ch])

    def __init__(self, dev_taps=()):
        self.taps = set(dev_taps)
        self.nc = bass.Bass("TRN2", target_bir_lowering=False)
        self.es = ExitStack()
        self.sems = {}
        self.counts = {}
        self._declare_dram()
        self._alloc_persistent()

    def sem(self, name):
        self.sems[name] = self.es.enter_context(self.nc.semaphore(name))

    def _declare_dram(self):
        nc = self.nc
        def di(name, shape, dt=F32):
            setattr(self, "d_" + name, nc.dram_tensor(name, list(shape), dt, kind="ExternalInput").ap())
        di("xt", (D, T)); di("wqkv_a", (D, 3072)); di("wout_t", (D, D)); di("bout_c", (128, 8))
        di("fwqkv", (2, D, 1536)); di("fwo", (2, 512, D)); di("fw1", (2, D, W1W))
        di("fb1", (2, 128, 2 * NHP)); di("fw2", (2, HIDP, D)); di("fb2", (2, 128, 8))
        di("flng", (2, 128, 8)); di("flnb", (2, 128, 8)); di("fqs", (128, 2)); di("fks", (128, 2))
        di("fqs_sw", (128, 2)); di("fks_sw", (128, 2))
        di("qcs", (128, T)); di("qss", (128, T)); di("kcs", (128, T)); di("kss", (128, T))
        di("ident", (128, 128), BF16); di("ones128b", (128, 1), BF16); di("onesf", (1, 128))
        di("qones", (128, 32), BF16); di("qsel", (8, 512), BF16)
        di("tri_lo", (128, 128)); di("tri_hi", (128, 128)); di("mlo", (128, 1280))
        di("gate", (128, 1)); di("rkprev", (1, 1), I32); di("rkself", (1, 1), I32); di("fcon", (128, len(FCONS)))
        self.d_out = nc.dram_tensor("outt", [D, T], F32, kind="ExternalOutput").ap()
        def dd(name, shape, dt):
            kind = "ExternalOutput" if ("bounce" in self.taps and name in (
                "qprime", "kprime_fm", "vprime_fm", "xacc")) else "Internal"
            setattr(self, "d_" + name, nc.dram_tensor(name, list(shape), dt, kind=kind).ap())
        dd("qprime", (D, T), BF16)
        dd("kprime_fm", (D, T), BF16)
        dd("vprime_fm", (D, T), BF16)
        dd("xacc", (D, T), F32)
        dd("agx_in", (D, 1), F32)
        dd("agx_out", (NCORES * D, 1), F32)
        dd("vbf", (512, T), BF16)
        dd("agkv_in", (2 * 512, CH), BF16)
        dd("agkv_out", (NCORES * 2 * 512, CH), BF16)
        dd("kv_in", (D, D), F32)
        dd("kvag1", (NCORES * D, D), F32)
        dd("kv2_in", (128, D), F32)
        dd("kv_out", (D, D), F32)
        dd("ks_in", (128, NT), F32)
        dd("ks_out", (NCORES * 128, NT), F32)
        if "halo" in self.taps:
            self.d_halo_tap = nc.dram_tensor("halo_tap", [NT, 128], F32, kind="ExternalOutput").ap()
        if "xacc" in self.taps:
            self.d_xacc_tap = nc.dram_tensor("xacc_tap", [D, T], F32, kind="ExternalOutput").ap()
        if "bounce" in self.taps:
            self.d_kv_tap = nc.dram_tensor("kv_tap", [D, D], F32, kind="ExternalOutput").ap()
            self.d_az_tap = nc.dram_tensor("az_tap", [D, T], BF16, kind="ExternalOutput").ap()
            self.d_z_tap = nc.dram_tensor("z_tap", [1, T], F32, kind="ExternalOutput").ap()
            self.d_ks_tap = nc.dram_tensor("ks_tap", [128, NT], F32, kind="ExternalOutput").ap()
            self.d_kvb_tap = nc.dram_tensor("kvb_tap", [D, D], BF16, kind="ExternalOutput").ap()
            self.d_kvin_tap = nc.dram_tensor("kvin_tap", [D, D], F32, kind="ExternalOutput").ap()
            self.d_kvin2_tap = nc.dram_tensor("kvin2_tap", [D, D], BF16, kind="ExternalOutput").ap()
            self.d_ksg_tap = nc.dram_tensor("ksg_tap", [128, NCORES * NT], F32, kind="ExternalOutput").ap()
            self.d_ksc_tap = nc.dram_tensor("ksc_tap", [128, NT], F32, kind="ExternalOutput").ap()
            self.d_ks4_tap = nc.dram_tensor("ks4_tap", [128, 4 * NT], F32, kind="ExternalOutput").ap()
        if "och" in self.taps:
            self.d_och_tap = nc.dram_tensor("och_tap", [128, NCH * 2048], BF16, kind="ExternalOutput").ap()
            self.d_kr_tap = nc.dram_tensor("kr_tap", [128, NCH * 2048], BF16, kind="ExternalOutput").ap()
            self.d_vr_tap = nc.dram_tensor("vr_tap", [128, NCH * 2048], BF16, kind="ExternalOutput").ap()
            self.d_qr_tap = nc.dram_tensor("qr_tap", [128, NCH * 2048], BF16, kind="ExternalOutput").ap()
            self.d_rn_tap = nc.dram_tensor("rn_tap", [128, NCH * 32], F32, kind="ExternalOutput").ap()
        for tp in ("p1", "a0", "b0", "a1", "b1"):
            if tp in self.taps:
                setattr(self, "d_tap_" + tp,
                        nc.dram_tensor("tap_" + tp, [D, T], F32, kind="ExternalOutput").ap())

    def _alloc_persistent(self):
        nc, es = self.nc, self.es
        sb = lambda name, shape, dt: es.enter_context(nc.sbuf_tensor(name, list(shape), dt))
        self.xf = [sb(f"xf{t}", (128, XW), F32) for t in range(NT)]
        self.xb = [sb(f"xb{t}", (128, XW), BF16) for t in range(NT)]
        self.NSLOT = 4
        self.ring = [sb(f"ring{i}", (128, 2048), BF16) for i in range(self.NSLOT)]
        self.r_mu = sb("r_mu", (1, CH), F32)
        self.r_rs = sb("r_rs", (1, CH), F32)
        self.r_z = sb("r_z", (1, CH), F32)
        self.c_ident = sb("c_ident", (128, 128), BF16)
        self.c_ones128b = sb("c_ones128b", (128, 1), BF16)
        self.c_onesf = sb("c_onesf", (1, 128), F32)
        self.c_qones = sb("c_qones", (128, 32), BF16)
        self.c_qsel = sb("c_qsel", (8, 512), BF16)
        self.c_trilo = sb("c_trilo", (128, 128), F32)
        self.c_trihi = sb("c_trihi", (128, 128), F32)
        self.c_mlo = sb("c_mlo", (128, 1280), F32)
        self.c_gate = sb("c_gate", (128, 1), F32)
        self.c_rk = sb("c_rk", (1, 1), I32)
        self.agxs = sb("agxs", (128, NT), F32)
        self.c_rs = sb("c_rs", (1, 1), I32)
        self.c_bout = sb("c_bout", (128, 8), F32)
        self.c_b1 = sb("c_b1", (128, 2 * 2 * NHP), F32)
        self.c_b2 = sb("c_b2", (128, 16), F32)
        self.c_lng = sb("c_lng", (128, 16), F32)
        self.c_lnb = sb("c_lnb", (128, 16), F32)
        self.c_qs = sb("c_qs", (128, 2), F32)
        self.c_ks = sb("c_ks", (128, 2), F32)
        self.c_qs_sw = sb("c_qs_sw", (128, 2), F32)
        self.c_ks_sw = sb("c_ks_sw", (128, 2), F32)
        self.c_fcon = sb("c_fcon", (128, len(FCONS)), F32)
        for i, v in enumerate(FCONS):
            nc.const_aps.register(F32, v, self.c_fcon[:, i:i + 1])
        for ch in ("dw", "rf", "ds", "dg", "do", "cc",
                   "pe", "dve", "act", "pool"):
            self.sem(ch)

    def ring_load(self, cx, src_ap, ncols):
        i = cx.counts.get("rn", 0)
        cx.bump("rn")
        slot = i % self.NSLOT
        if i >= self.NSLOT:
            cx.wabs(cx.pool, "rf", i - self.NSLOT + 1)
        ins = cx.g.dma_start(out=self.ring[slot][:, 0:ncols], in_=src_ap) if cx.pool else None
        cx.inc("dw", ins, 16)
        return slot, i

    def run_block(self, *bodies):
        base = dict(self.counts)
        finals = {}
        with self.nc.Block() as block:
            for role in ROLES:
                counts = dict(base)
                deco = getattr(block, _R2A[role])
                def make(role, counts):
                    def f(eng):
                        cx = Cx(self, role, eng, counts)
                        for body in bodies:
                            body(cx)
                    return f
                deco(make(role, counts))
                finals[role] = counts
        ref = finals[ROLES[0]]
        for role in ROLES[1:]:
            assert finals[role] == ref, f"count divergence {role}: " + str(
                {k: (finals[role].get(k), ref.get(k)) for k in set(finals[role]) | set(ref)
                 if finals[role].get(k) != ref.get(k)})
        self.counts = ref

    _psid = 0

    def block(self, psums, *bodies):
        with ExitStack() as pes:
            self.ps = {}
            for name, (shape, dt) in psums.items():
                Build._psid += 1
                self.ps[name] = pes.enter_context(
                    self.nc.psum_tensor(f"ps_{name}_{Build._psid}", list(shape), dt))
            self.run_block(*bodies)

    def ph_consts(self, cx):
        def ld(dst, src):
            ins = cx.q.dma_start(out=dst, in_=src) if cx.sp else None
            cx.inc("ds", ins, 16)
        for t in range(NT):
            ld(self.xf[t][:, 1:1 + T], self.d_xt[128 * t:128 * t + 128, :])
        ld(self.c_ident[:, :], self.d_ident[:, :])
        ld(self.c_ones128b[:, :], self.d_ones128b[:, :])
        ld(self.c_onesf[:, :], self.d_onesf[:, :])
        ld(self.c_qones[:, :], self.d_qones[:, :])
        ld(self.c_qsel[:, :], self.d_qsel[:, :])
        ld(self.c_trilo[:, :], self.d_tri_lo[:, :])
        ld(self.c_trihi[:, :], self.d_tri_hi[:, :])
        ld(self.c_mlo[:, :], self.d_mlo[:, :])
        ld(self.c_gate[:, :], self.d_gate[:, :])
        ld(self.c_rk[:, :], self.d_rkprev[:, :])
        ld(self.c_rs[:, :], self.d_rkself[:, :])
        ld(self.c_bout[:, :], self.d_bout_c[:, :])
        for l in range(2):
            ld(self.c_b1[:, 44 * l:44 * l + 44], self.d_fb1[l, :, :])
            ld(self.c_b2[:, 8 * l:8 * l + 8], self.d_fb2[l, :, :])
            ld(self.c_lng[:, 8 * l:8 * l + 8], self.d_flng[l, :, :])
            ld(self.c_lnb[:, 8 * l:8 * l + 8], self.d_flnb[l, :, :])
        ld(self.c_qs[:, :], self.d_fqs[:, :])
        ld(self.c_ks[:, :], self.d_fks[:, :])
        ld(self.c_qs_sw[:, :], self.d_fqs_sw[:, :])
        ld(self.c_ks_sw[:, :], self.d_fks_sw[:, :])
        ld(self.c_fcon[:, :], self.d_fcon[:, :])
        for t in range(NT):
            i1 = cx.v.memset(self.xf[t][:, 0:1], 0.0) if cx.dve else None
            cx.inc("dve", i1)
            i2 = cx.v.memset(self.xb[t][:, 0:1], 0.0) if cx.dve else None
            cx.inc("dve", i2)
        if cx.sp:
            cx.eng.wait_ge(self.b_sem("ds"), cx.counts["ds"])

    def b_sem(self, name):
        return self.sems[name]

    def ph_p1(self, cx):
        ps = self.ps
        nds = 16 * (NT)
        for c in range(NCH):
            col = 1 + CH * c
            for t in range(NT):
                sq = self.sqb[t % 2]
                cx.wcur(cx.act, "ds")
                if c * NT + t >= 2:
                    cx.wcur(cx.act, "pe")
                i = cx.s.copy(self.xb[t][:, col:col + CH], self.xf[t][:, col:col + CH]) if cx.act else None
                cx.inc("act", i)
                i = cx.s.activation(sq[:, :], self.xf[t][:, col:col + CH], AF.Square) if cx.act else None
                cx.inc("act", i)
                cx.wcur(cx.pe, "act")
                i = cx.t.matmul(ps["sx"][:, :], self.c_ones128b[:, :],
                                self.xb[t][:, col:col + CH], start=(t == 0), stop=(t == NT - 1)) if cx.pe else None
                cx.inc("pe", i)
                i = cx.t.matmul(ps["sx2"][:, :], self.c_ones128b[:, :],
                                sq[:, :], start=(t == 0), stop=(t == NT - 1)) if cx.pe else None
                cx.inc("pe", i)
            cx.wcur(cx.dve, "pe")
            i = cx.v.tensor_scalar_mul(self.r_mu[:, :], ps["sx"][:, :], 1.0 / D) if cx.dve else None
            cx.inc("dve", i)
            cx.wcur(cx.act, "dve")
            i = cx.s.activation(self.rowa[:, :], self.r_mu[:, :], AF.Square) if cx.act else None
            cx.inc("act", i)
            cx.wcur(cx.dve, "act")
            i = cx.v.scalar_tensor_tensor(self.rowb[:, :], ps["sx2"][:, :], 1.0 / D, self.rowa[:, :],
                                          OP.mult, OP.subtract) if cx.dve else None
            cx.inc("dve", i)
            cx.wcur(cx.act, "dve")
            i = cx.s.activation(self.rowa[:, :], self.rowb[:, :], AF.Ln, bias=1e-5) if cx.act else None
            cx.inc("act", i)
            i = cx.s.activation(self.r_rs[:, :], self.rowa[:, :], AF.Exp, scale=-0.5) if cx.act else None
            cx.inc("act", i)
            cx.wcur(cx.pe, "act")
            i = cx.t.matmul(ps["mu"][:, :], self.c_onesf[:, :], self.r_mu[:, :],
                            start=True, stop=True) if cx.pe else None
            cx.inc("pe", i)
            i = cx.t.matmul(ps["rs"][:, :], self.c_onesf[:, :], self.r_rs[:, :],
                            start=True, stop=True) if cx.pe else None
            cx.inc("pe", i)
            for t in range(NT):
                cx.wcur(cx.dve, "pe")
                i = cx.v.scalar_tensor_tensor(self.st5[t % 2][:, :], ps["mu"][:, :], -1.0,
                                              self.xf[t][:, col:col + CH], OP.mult, OP.add) if cx.dve else None
                cx.inc("dve", i)
                i = cx.v.tensor_mul(self.xf[t][:, col:col + CH], self.st5[t % 2][:, :], ps["rs"][:, :]) if cx.dve else None
                cx.inc("dve", i)
                cx.wcur(cx.act, "dve")
                i = cx.s.copy(self.xb[t][:, col:col + CH], self.xf[t][:, col:col + CH]) if cx.act else None
                cx.inc("act", i)
            cx.wcur(cx.pe, "act")
            cx.wcur(cx.pe, "dve")
        self._tap(cx, "p1")

    def _tap(self, cx, name):
        if name not in self.taps:
            return
        d = getattr(self, "d_tap_" + name)
        cx.wcur(cx.sp, "dve")
        cx.wcur(cx.sp, "act")
        cx.wcur(cx.sp, "pe")
        for t in range(NT):
            i = cx.q.dma_start(out=d[128 * t:128 * t + 128, :], in_=self.xf[t][:, 1:1 + T]) if cx.sp else None
            cx.inc("do", i, 16)
        cx.wcur(cx.sp, "do")

    def ph_finale(self, cx, with_xacc):
        do0 = cx.counts.get("do", 0)
        for c in range(NCH):
            col = 1 + CH * c
            for t in range(NT):
                if with_xacc:
                    it = c * NT + t
                    xa = self.sqf[it % 2]
                    if it >= 2:
                        cx.wcur(cx.sp, "dve")
                    ins = cx.q.dma_start(out=xa[:, :], in_=self.d_xacc[128 * t:128 * t + 128, CH * c:CH * c + CH]) if cx.sp else None
                    cx.inc("ds", ins, 16)
                    if it >= 2:
                        cx.wcur(cx.dve, "do")
                    cx.wcur(cx.dve, "ds")
                    i = cx.v.tensor_add(self.st5[it % 2][:, :], self.xf[t][:, col:col + CH], xa[:, :]) if cx.dve else None
                    cx.inc("dve", i)
                    cx.wcur(cx.sp, "dve")
                    ins = cx.q.dma_start(out=self.d_out[128 * t:128 * t + 128, CH * c:CH * c + CH], in_=self.st5[it % 2][:, :]) if cx.sp else None
                    cx.inc("do", ins, 16)
                else:
                    i = cx.q.dma_start(out=self.d_out[128 * t:128 * t + 128, CH * c:CH * c + CH],
                                       in_=self.xf[t][:, col:col + CH]) if cx.sp else None
                    cx.inc("do", i, 16)
        cx.wcur(cx.sp, "do")


    def wload(self, cx, tensor, elem_off, row_stride, nk, ncols):
        i = cx.counts.get("rn", 0)
        cx.bump("rn")
        slot = i % self.NSLOT
        if i >= self.NSLOT:
            cx.wabs(cx.pool, "pe", cx.counts[f"rpe{i - self.NSLOT}"])
        ins = None
        if cx.pool:
            src = bass.AP(tensor, elem_off,
                          [[row_stride, 128], [128 * row_stride, nk], [1, ncols]])
            ins = cx.g.dma_start(out=self.ring[slot][:, 0:nk * ncols], in_=src)
        cx.inc(f"dw{slot}", ins, 16)
        return slot, i

    def ring_mark(self, cx, li):
        cx.counts[f"rpe{li}"] = cx.counts.get("pe", 0)


    NO_AGX = False

    def ph_p2(self, cx):
        if not Build.NO_AGX:
            self.agx_export(cx)
        qk = [self.ps["qk0"], self.ps["qk1"]]
        for c in range(NCH):
            col = 1 + CH * c
            evA0 = cx.counts.get("evA", 0)
            stD_now = cx.counts.get("stD", 0)
            for mt in range(24):
                self.sync_pt(cx)
                slot, li = self.wload(cx, self.d_wqkv_a.tensor, 128 * mt, 3072, NT, 128)
                if mt >= 2:
                    cx.wabs(cx.pe, "evA", evA0 + mt - 1)
                elif c > 0:
                    cx.wabs(cx.pe, "evA", evA0 - 1 + mt)
                cx.wabs(cx.pe, f"dw{slot}", 16 * (li // self.NSLOT + 1))
                pb = qk[mt % 2]
                last = None
                for k in range(NT):
                    ins = cx.t.matmul(pb[:, :], self.ring[slot][:, 128 * k:128 * k + 128],
                                      self.xb[k][:, col:col + CH],
                                      start=(k == 0), stop=(k == NT - 1)) if cx.pe else None
                    last = ins or last
                cx.inc("pe", last)
                self.ring_mark(cx, li)
                if mt < 16:
                    stage = self.qstage if mt < 8 else self.kstage
                    off = 512 * (mt % 8)
                    cx.wcur(cx.act, "pe")
                    cx.wcur(cx.act, "dve")
                    if c > 0:
                        cx.wabs(cx.act, "stD", stD_now)
                    i2 = cx.s.activation(self.st5[mt % 2][:, :], pb[:, :], AF.Relu) if cx.act else None
                    cx.inc("evA", i2)
                    cx.wcur(cx.dve, "evA")
                    i3 = cx.v.tensor_scalar_add(stage[:, off:off + CH], self.st5[mt % 2][:, :], 1e-6) if cx.dve else None
                    cx.inc("dve", i3)
                    if 8 <= mt:
                        kt = mt - 8
                        i4 = cx.v.reduce_sum(self.kred[:, :], self.st5[mt % 2][:, :], AX.X) if cx.dve else None
                        cx.inc("dve", i4)
                        cx.wcur(cx.dve, "dve")
                        i5 = cx.v.tensor_copy(self.ksum4[:, 8 * c + kt:8 * c + kt + 1], self.kred[:, :]) if cx.dve else None
                        cx.inc("dve", i5)
                else:
                    off = 512 * (mt - 16)
                    cx.wcur(cx.dve, "pe")
                    if c > 0:
                        cx.wabs(cx.dve, "stD", stD_now)
                    i2 = cx.v.tensor_copy(self.vstage[:, off:off + CH], pb[:, :]) if cx.dve else None
                    cx.inc("evA", i2)
            cx.wcur(cx.sp, "dve")
            cx.wcur(cx.sp, "evA")
            for stage, dram in ((self.qstage, self.d_qprime), (self.kstage, self.d_kprime_fm),
                                (self.vstage, self.d_vprime_fm)):
                ins = None
                if cx.sp:
                    dst = bass.AP(dram.tensor, CH * c, [[T, 128], [128 * T, NT], [1, CH]])
                    ins = cx.q.dma_start(out=dst, in_=stage[:, :])
                cx.inc("stD", ins, 16)
        cx.wcur(cx.dve, "dve")
        i = cx.v.tensor_add(self.ksumt[:, 0:8], self.ksum4[:, 0:8], self.ksum4[:, 8:16]) if cx.dve else None
        cx.inc("dve", i)
        i = cx.v.tensor_add(self.ksumt[:, 8:16], self.ksum4[:, 16:24], self.ksum4[:, 24:32]) if cx.dve else None
        cx.inc("dve", i)
        cx.wcur(cx.dve, "dve")
        i = cx.v.tensor_add(self.ksumc[:, :], self.ksumt[:, 0:8], self.ksumt[:, 8:16]) if cx.dve else None
        cx.inc("dve", i)
        cx.wcur(cx.dve, "dve")
        i = cx.v.tensor_scalar_add(self.ksumc[:, :], self.ksumc[:, :], T * 1e-6) if cx.dve else None
        cx.inc("dve", i)
        cx.wcur(cx.dve, "dve")
        cx.wcur(cx.sp, "dve")
        ins = cx.q.dma_start(out=self.d_ks_in[:, :], in_=self.ksumc[:, :]) if cx.sp else None
        cx.inc("ksD", ins, 16)
        if "bounce" in self.taps:
            ins = cx.q.dma_start(out=self.d_ksc_tap[:, :], in_=self.ksumc[:, :]) if cx.sp else None
            cx.inc("ksD", ins, 16)
            ins = cx.q.dma_start(out=self.d_ks4_tap[:, :], in_=self.ksum4[:, :]) if cx.sp else None
            cx.inc("ksD", ins, 16)
        cx.wcur(cx.sp, "stD")

    def agx_export(self, cx):
        cx.wcur(cx.sp, "dve")
        cx.wcur(cx.sp, "act")
        for t in range(NT):
            ins = cx.q.dma_start(out=self.d_agx_in[128 * t:128 * t + 128, 0:1],
                                 in_=self.xf[t][:, T:T + 1]) if cx.sp else None
            cx.inc("ds", ins, 16)
        cx.wcur(cx.pool, "ds")
        ins = None
        if cx.pool:
            ins = cx.g.collective_compute(
                "AllGather", OP.bypass, replica_groups=[list(range(NCORES))],
                ins=[self.d_agx_in[:, :].opt()], outs=[self.d_agx_out[:, :].opt()])
        cx.inc("cc", ins)
        cx.wcur(cx.sp, "cc")
        ins = None
        if cx.sp:
            rk = self.nc.values_load(self.c_rk[0:1, 0:1], engines=[mybir.EngineType.SP])
            srcap = bass.AP(self.d_agx_out.tensor, rk * D, [[1, 128], [128, NT], [1, 1]])
            ins = cx.q.dma_start(out=self.agxs[:, :].rearrange("p (t c) -> p t c", c=1),
                                 in_=srcap, allow_slow_non_contiguous=True)
        cx.inc("ds", ins, 16)
        cx.wcur(cx.dve, "ds")
        i = cx.v.tensor_mul(self.agxs[:, :], self.agxs[:, :], self.c_gate[:, 0:1].broadcast_to([128, NT])) if cx.dve else None
        cx.inc("dve", i)
        cx.wcur(cx.dve, "dve")
        for t in range(NT):
            i = cx.v.tensor_copy(self.xf[t][:, 0:1], self.agxs[:, t:t + 1]) if cx.dve else None
            cx.inc("dve", i)
        cx.wcur(cx.act, "dve")
        for t in range(NT):
            i = cx.s.copy(self.xb[t][:, 0:1], self.xf[t][:, 0:1]) if cx.act else None
            cx.inc("act", i)
        if "halo" in self.taps:
            cx.wcur(cx.sp, "act")
            for t in range(NT):
                ins = cx.q.dma_start(out=self.d_halo_tap[t:t + 1, :].rearrange("a (b c) -> (a b) c", b=128), in_=self.xf[t][:, 0:1]) if cx.sp else None
                cx.inc("do", ins, 16)
            cx.wcur(cx.sp, "do")

    def ph_p3(self, cx):
        kv = [self.ps[f"kv{m}"] for m in range(4)]
        self._kvD0 = cx.counts.get("ds", 0)
        for h in range(2):
            pe0 = cx.counts.get("pe", 0)
            for tt in range(16):
                sl = tt % 2
                if tt >= 2:
                    cx.wabs(cx.sp, "pe", pe0 + tt - 1)
                elif h == 1:
                    cx.wabs(cx.sp, "pe", pe0)
                ins = None
                if cx.sp:
                    ins = cx.q.dma_start_transpose(
                        self.ktm[sl][:, :],
                        self.d_kprime_fm[512 * h:512 * h + 512, 128 * tt:128 * tt + 128])
                cx.inc("trD", ins, 16)
                ins = None
                if cx.sp:
                    ins = cx.q.dma_start_transpose(
                        self.vtm[sl][:, :],
                        self.d_vprime_fm[:, 128 * tt:128 * tt + 128])
                cx.inc("trD", ins, 16)
                cx.wcur(cx.pe, "trD")
                if h == 1 and tt == 0:
                    cx.wcur(cx.pe, "kvD0")
                    cx.wcur(cx.pe, "kvD1")
                last = None
                for m in range(4):
                    for nh in range(2):
                        ins = cx.t.matmul(kv[m][:, 512 * nh:512 * nh + 512],
                                          self.ktm[sl][:, 128 * m:128 * m + 128],
                                          self.vtm[sl][:, 512 * nh:512 * nh + 512],
                                          start=(tt == 0), stop=(tt == 15)) if cx.pe else None
                        last = ins or last
                cx.inc("pe", last)
            cx.wcur(cx.dve, "pe")
            nD = 32 if "bounce" in self.taps else 16
            for m in range(4):
                j = 4 * h + m
                if j >= 2:
                    cx.wabs(cx.dve, f"kvD{j % 2}", nD * (j // 2))
                i = cx.v.tensor_copy(self.kvstage[m % 2][:, :], kv[m][:, :]) if cx.dve else None
                cx.inc("dve", i)
                cx.wcur(cx.sp, "dve")
                ins = cx.q.dma_start(out=self.d_kv_in[512 * h + 128 * m:512 * h + 128 * m + 128, 0:D],
                                     in_=self.kvstage[m % 2][:, :]) if cx.sp else None
                cx.inc(f"kvD{j % 2}", ins, 16)
                if "bounce" in self.taps:
                    ins = cx.q.dma_start(out=self.d_kvin_tap[512 * h + 128 * m:512 * h + 128 * m + 128, :],
                                         in_=self.kvstage[m % 2][:, :]) if cx.sp else None
                    cx.inc(f"kvD{j % 2}", ins, 16)
        cx.wcur(cx.pool, "kvD0")
        cx.wcur(cx.pool, "kvD1")
        cx.wcur(cx.pool, "ksD")
        ins = None
        if cx.pool:
            ins = cx.g.collective_compute(
                "AllGather", OP.bypass, replica_groups=[list(range(NCORES))],
                ins=[self.d_ks_in[:, :].opt()], outs=[self.d_ks_out[:, :].opt()])
        cx.inc("cc", ins)
        ins = None
        if cx.pool:
            ins = cx.g.collective_compute(
                "AllGather", OP.bypass, replica_groups=[list(range(NCORES))],
                ins=[self.d_kv_in[:, :].opt()], outs=[self.d_kvag1[:, :].opt()])
        cx.inc("cc", ins)

    def ph_p3r(self, cx):
        cx.wcur(cx.sp, "cc")
        ins = None
        if cx.sp:
            rs = self.nc.values_load(self.c_rs[0:1, 0:1], engines=[mybir.EngineType.SP])
            srcap = bass.AP(self.d_kvag1.tensor, (rs * 128) * D,
                            [[D, 128], [D * D, NCORES], [1, D]])
            ins = cx.q.dma_start(out=self.agldt[:, :].rearrange("p (r c) -> p r c", c=D), in_=srcap)
        cx.inc("agD", ins, 16)
        cx.wcur(cx.dve, "agD")
        L = lambda r: self.agldt[:, D * r:D * r + D]
        pairs = [(self.agu[0], L(0), L(1)),
                 (self.agu[1], L(2), L(3)),
                 (self.agu[2], L(4), L(5)),
                 (self.agu[3], L(6), L(7)),
                 (L(0), self.agu[0], self.agu[1]),
                 (L(1), self.agu[2], self.agu[3]),
                 (self.agu[0], L(0), L(1))]
        for (o, a, bb) in pairs:
            i = cx.v.tensor_add(o[:, :], a[:, :], bb[:, :]) if cx.dve else None
            cx.inc("dve", i)
        cx.wcur(cx.sp, "dve")
        ins = cx.q.dma_start(out=self.d_kv2_in[:, :], in_=self.agu[0][:, :]) if cx.sp else None
        cx.inc("agD", ins, 16)
        cx.wcur(cx.pool, "agD")
        ins = None
        if cx.pool:
            ins = cx.g.collective_compute(
                "AllGather", OP.bypass, replica_groups=[list(range(NCORES))],
                ins=[self.d_kv2_in[:, :].opt()], outs=[self.d_kv_out[:, :].opt()])
        cx.inc("cc", ins)

    DELAY_AFTER_CC = False

    def ph_af(self, cx):
        cx.wcur(cx.pool, "cc")
        cx.wcur(cx.sp, "cc")
        if Build.DELAY_AFTER_CC:
            for dd in range(8):
                ins = cx.g.dma_start(out=self.d_qprime[:, 0:T], in_=self.d_qprime[:, 0:T]) if cx.pool else None
                cx.inc("dg", ins, 16)
                cx.wcur(cx.pool, "dg")
        if "bounce" in self.taps:
            ins = cx.g.dma_start(out=self.d_kv_tap[:, :], in_=self.d_kv_out[:, :]) if cx.pool else None
            cx.inc("dg", ins, 16)
        for t in range(NT):
            ins = cx.g.dma_start(out=self.kvb[t][:, :],
                                 in_=self.d_kv_out[128 * t:128 * t + 128, :]) if cx.pool else None
            cx.inc("dg", ins, 16)
        for r in range(NCORES):
            ins = cx.q.dma_start(out=self.ksg[:, NT * r:NT * r + NT],
                                 in_=self.d_ks_out[128 * r:128 * r + 128, :]) if cx.sp else None
            cx.inc("qD", ins, 16)
        cx.wcur(cx.dve, "qD")
        if "bounce" in self.taps:
            cx.wcur(cx.sp, "qD")
            ins = cx.q.dma_start(out=self.d_ksg_tap[:, :], in_=self.ksg[:, :]) if cx.sp else None
            cx.inc("do", ins, 16)
        for st, (a, bb) in enumerate(((0, 1), (2, 3), (4, 5), (6, 7))):
            i = cx.v.tensor_add(self.ksg2[:, NT * st:NT * st + NT],
                                self.ksg[:, NT * a:NT * a + NT],
                                self.ksg[:, NT * bb:NT * bb + NT]) if cx.dve else None
            cx.inc("dve", i)
        cx.wcur(cx.dve, "dve")
        i = cx.v.tensor_add(self.ksg[:, 0:NT], self.ksg2[:, 0:NT], self.ksg2[:, NT:2 * NT]) if cx.dve else None
        cx.inc("dve", i)
        i = cx.v.tensor_add(self.ksg[:, NT:2 * NT], self.ksg2[:, 2 * NT:3 * NT], self.ksg2[:, 3 * NT:4 * NT]) if cx.dve else None
        cx.inc("dve", i)
        cx.wcur(cx.dve, "dve")
        i = cx.v.tensor_add(self.ksumb[:, :], self.ksg[:, 0:NT], self.ksg[:, NT:2 * NT]) if cx.dve else None
        cx.inc("dve", i)
        cx.wcur(cx.dve, "dve")
        if "bounce" in self.taps:
            i = cx.v.tensor_copy(self.kstap_sb[:, :], self.ksumb[:, :]) if cx.dve else None
            cx.inc("dve", i)
            cx.wcur(cx.sp, "dve")
            ins = cx.q.dma_start(out=self.d_ks_tap[:, :], in_=self.kstap_sb[:, :]) if cx.sp else None
            cx.inc("do", ins, 16)
        at = [self.ps["at0"], self.ps["at1"]]
        wo = [self.ps["wo0"], self.ps["wo1"]]
        for c in range(NCH):
            cx.counts["afc"] = c
            cx.wcur(cx.sp, "pe")
            for t in range(NT):
                ins = None
                if cx.sp:
                    ins = cx.q.dma_start(out=self.qch[t][:, :],
                                         in_=self.d_qprime[128 * t:128 * t + 128, CH * c:CH * c + CH])
                cx.inc("qD", ins, 16)
            cx.wcur(cx.pe, "qD")
            cx.wcur(cx.pe, "dg")
            last = None
            for k in range(NT):
                ins = cx.t.matmul(self.ps["zden"][:, :], self.ksumb[:, k:k + 1],
                                  self.qch[k][:, :], start=(k == 0), stop=(k == NT - 1)) if cx.pe else None
                last = ins or last
            cx.inc("pe", last)
            cx.wcur(cx.dve, "pe")
            i = cx.v.tensor_scalar_add(self.rowa[:, :], self.ps["zden"][:, :], 1e-6) if cx.dve else None
            cx.inc("dve", i)
            i = cx.v.reciprocal(self.r_z[:, :], self.rowa[:, :]) if cx.dve else None
            cx.inc("dve", i)
            cx.wcur(cx.pe, "dve")
            i = cx.t.matmul(self.ps["zbc"][:, :], self.c_onesf[:, :], self.r_z[:, :],
                            start=True, stop=True) if cx.pe else None
            cx.inc("pe", i)
            cx.wcur(cx.dve, "pe")
            i = cx.v.tensor_copy(self.zbc_sb[:, :], self.ps["zbc"][:, :]) if cx.dve else None
            cx.inc("dve", i)
            cx.wcur(cx.pe, "dg")
            evZ0 = cx.counts.get("evZ", 0)
            for et in range(NT):
                if et >= 2:
                    cx.wabs(cx.pe, "evZ", evZ0 + et - 1)
                pb = at[et % 2]
                last = None
                for k in range(NT):
                    ins = cx.t.matmul(pb[:, :], self.kvb[k][:, 128 * et:128 * et + 128],
                                      self.qch[k][:, :], start=(k == 0), stop=(k == NT - 1)) if cx.pe else None
                    last = ins or last
                cx.inc("pe", last)
                cx.wcur(cx.dve, "pe")
                i = cx.v.scalar_tensor_tensor(self.azb[et][:, :], pb[:, :], 1.0,
                                              self.zbc_sb[:, :], OP.mult, OP.mult) if cx.dve else None
                cx.inc("evZ", i)
            if "bounce" in self.taps:
                cx.wcur(cx.dve, "dve")
                i = cx.v.tensor_copy(self.ztap_sb[:, CH * c:CH * c + CH], self.r_z[:, :]) if cx.dve else None
                cx.inc("dve", i)
            evW0 = cx.counts.get("evW", 0)
            for ft in range(NT):
                slot, li = self.wload(cx, self.d_wout_t.tensor, 128 * ft, D, NT, 128)
                cx.wabs(cx.pe, f"dw{slot}", 16 * (li // self.NSLOT + 1))
                if ft >= 2:
                    cx.wabs(cx.pe, "evW", evW0 + ft - 1)
                cx.wcur(cx.pe, "evZ")
                pb = wo[ft % 2]
                last = None
                for k in range(NT):
                    ins = cx.t.matmul(pb[:, :], self.ring[slot][:, 128 * k:128 * k + 128],
                                      self.azb[k][:, :], start=(k == 0), stop=(k == NT - 1)) if cx.pe else None
                    last = ins or last
                cx.inc("pe", last)
                self.ring_mark(cx, li)
                if ft >= 2:
                    cx.wabs(cx.sp, "evW", evW0 + ft - 1)
                ins = None
                if cx.sp:
                    ins = cx.q.dma_start(out=self.sqf[ft % 2][:, :],
                                         in_=self.d_xt[128 * ft:128 * ft + 128, CH * c:CH * c + CH])
                cx.inc("xD", ins, 16)
                if ft >= 2:
                    cx.wabs(cx.dve, f"xa{ft % 2}", 16 * (cx.counts.get("afc", 0) * 4 + ft // 2))
                cx.wcur(cx.dve, "pe")
                cx.wcur(cx.dve, "xD")
                i = cx.v.scalar_tensor_tensor(self.xastage[ft % 2][:, :], pb[:, :],
                                              self.c_bout[:, ft:ft + 1], self.sqf[ft % 2][:, :],
                                              OP.add, OP.add) if cx.dve else None
                cx.inc("evW", i)
                cx.wcur(cx.sp, "evW")
                ins = cx.q.dma_start(out=self.d_xacc[128 * ft:128 * ft + 128, CH * c:CH * c + CH],
                                     in_=self.xastage[ft % 2][:, :]) if cx.sp else None
                cx.inc(f"xa{ft % 2}", ins, 16)
        cx.wabs(cx.sp, "xa0", 16 * NCH * 4)
        cx.wabs(cx.sp, "xa1", 16 * NCH * 4)
        if "bounce" in self.taps:
            for t in range(NT):
                ins = cx.g.dma_start(out=self.kvb[t][:, :], in_=self.d_kv_in[128 * t:128 * t + 128, :]) if cx.pool else None
                cx.inc("dg", ins, 16)
            cx.wcur(cx.sp, "dg")
            for t in range(NT):
                ins = cx.q.dma_start(out=self.d_kvin2_tap[128 * t:128 * t + 128, :],
                                     in_=self.kvb[t][:, :]) if cx.sp else None
                cx.inc("do", ins, 16)
            cx.wcur(cx.sp, "do")
            cx.wcur(cx.pool, "do")
            for t in range(NT):
                ins = cx.g.dma_start(out=self.kvb[t][:, :], in_=self.d_kv_out[128 * t:128 * t + 128, :]) if cx.pool else None
                cx.inc("dg", ins, 16)
            cx.wcur(cx.sp, "dg")
            cx.wcur(cx.sp, "dve")
            ins = cx.q.dma_start(out=self.d_z_tap[:, :], in_=self.ztap_sb[:, :]) if cx.sp else None
            cx.inc("do", ins, 16)
            for t in range(NT):
                ins = cx.q.dma_start(out=self.d_kvb_tap[128 * t:128 * t + 128, :],
                                     in_=self.kvb[t][:, :]) if cx.sp else None
                cx.inc("do", ins, 16)
            cx.wcur(cx.sp, "do")
        if "xacc" in self.taps:
            for t in range(NT):
                ins = cx.q.dma_start(out=self.d_xacc_tap[128 * t:128 * t + 128, :],
                                     in_=self.d_xacc[128 * t:128 * t + 128, :]) if cx.sp else None
                cx.inc("do", ins, 16)
            cx.wcur(cx.sp, "do")


    KSLOT = {-1: 0, 0: 1, 1: 2, 2: 3, 3: 2}

    A_PARTS = ("attn", "wo")

    def ph_A(self, cx, l):
        pp = [self.ps["pp0"], self.ps["pp1"]]
        it_ix = [0]
        base = {ch: cx.counts.get(ch, 0) for ch in ("ex", "tr", "te", "av", "sc", "oe")}
        at = "attn" in Build.A_PARTS
        wo = "wo" in Build.A_PARTS

        self.sub_qkvrot(cx, l, 0, pp)
        self.sub_qkvrot(cx, l, 1, pp)
        if at: self.sub_attn(cx, l, 1, it_ix, base)
        if wo: self.sub_wo(cx, l, 1, pp)
        self.sub_qkvrot(cx, l, 2, pp)
        if at: self.sub_attn(cx, l, 2, it_ix, base)
        if wo: self.sub_wo(cx, l, 2, pp)
        self.sub_qkvrot(cx, l, 3, pp)
        self.sub_agkv(cx, l)
        if at: self.sub_attn(cx, l, 3, it_ix, base)
        if wo: self.sub_wo(cx, l, 3, pp)
        self.agx_export(cx)
        self.sub_agkv_read(cx, l)
        if at: self.sub_attn(cx, l, 0, it_ix, base)
        if wo: self.sub_wo(cx, l, 0, pp)

    def sub_qkvrot(self, cx, l, c, pp):
        col = 1 + CH * c
        woff = l * D * 1536
        qslot = self.qrot[0 if c == 0 else 1]
        kslot = self.kroll[Build.KSLOT[c]]
        vslot = self.vroll[Build.KSLOT[c]]
        evq0 = cx.counts.get("evQ", 0)
        cx.wcur(cx.dve, "sm")
        cx.wcur(cx.sp, "sm")
        for mi, mt in enumerate([0, 1, 2, 3, 8, 9, 10, 11, 4, 5, 6, 7]):
            slot, li = self.wload(cx, self.d_fwqkv.tensor, woff + 128 * mt, 1536, NT, 128)
            cx.wabs(cx.pe, f"dw{slot}", 16 * (li // self.NSLOT + 1))
            if mi >= 2:
                cx.wabs(cx.pe, "evQ", evq0 + mi - 1)
            else:
                cx.wcur(cx.pe, "evQ")
                cx.wcur(cx.pe, "oe")
            pb = pp[mi % 2]
            last = None
            for k in range(NT):
                sh = 1 if k < 4 else 0
                ins = cx.t.matmul(pb[:, :], self.ring[slot][:, 128 * k:128 * k + 128],
                                  self.xb[k][:, col - 1 + sh:col - 1 + sh + CH],
                                  start=(k == 0), stop=(k == NT - 1)) if cx.pe else None
                last = ins if ins is not None else last
            cx.inc("pe", last)
            self.ring_mark(cx, li)
            if mt < 8:
                ft = mt % 4
                cx.wcur(cx.dve, "pe")
                i = cx.v.tensor_copy(self.ust[ft][:, :], pb[:, :]) if cx.dve else None
                cx.inc("evQ", i)
            else:
                vt = mt - 8
                cx.wcur(cx.dve, "pe")
                cx.wabs(cx.dve, "vbD", cx.counts.get("vbD", 0))
                i = cx.v.tensor_copy(self.vfm[:, :], pb[:, :]) if cx.dve else None
                cx.inc("evQ", i)
                cx.wcur(cx.sp, "evQ")
                ins = None
                if cx.sp:
                    dst = self.d_vbf[128 * vt:128 * vt + 128, CH * c:CH * c + CH]
                    ins = cx.q.dma_start(out=dst, in_=self.vfm[:, :])
                cx.inc("vbD", ins, 16)
            if mi == 7:
                self.sub_rot(cx, l, c, "q", qslot)
        self.sub_rot(cx, l, c, "k", kslot)
        cx.wcur(cx.sp, "vbD")
        cx.wcur(cx.sp, "av")
        for tt in range(4):
            ins = None
            if cx.sp:
                ins = cx.q.dma_start_transpose(
                    self.vtm_view(vslot, tt),
                    self.d_vbf[:, CH * c + 128 * tt:CH * c + 128 * tt + 128])
            cx.inc("trD", ins, 16)
        if "och" in self.taps and l == 0:
            cx.wcur(cx.sp, "rnev")
            cx.wcur(cx.sp, "dve")
            cx.wcur(cx.sp, "trD")
            rq = self.rnqT[0 if c == 0 else 1]
            for dst, src in ((self.d_kr_tap, kslot[:, :]), (self.d_vr_tap, vslot[:, :]),
                             (self.d_qr_tap, qslot[:, :])):
                ins = cx.q.dma_start(out=dst[:, 2048 * c:2048 * c + 2048], in_=src) if cx.sp else None
                cx.inc("do", ins, 16)
            ins = cx.q.dma_start(out=self.d_rn_tap[:, 32 * c:32 * c + 32], in_=rq[:, :]) if cx.sp else None
            cx.inc("do", ins, 16)
            cx.wcur(cx.dve, "do")
            cx.wcur(cx.pe, "do")

    def vtm_view(self, vslot, tt):
        return vslot[:, 512 * tt:512 * tt + 512]

    def sub_rot(self, cx, l, c, qk, dstslot):
        tabs = (self.d_qcs, self.d_qss) if qk == "q" else (self.d_kcs, self.d_kss)
        cx.wcur(cx.sp, "dve")
        for i2, tab in enumerate(tabs):
            ins = cx.q.dma_start(out=self.tab[i2][:, :], in_=tab[:, CH * c:CH * c + CH]) if cx.sp else None
            cx.inc("tabD", ins, 16)
        scol = self.c_qs[:, l:l + 1] if qk == "q" else self.c_ks[:, l:l + 1]
        ssw = self.c_qs_sw[:, l:l + 1] if qk == "q" else self.c_ks_sw[:, l:l + 1]
        cx.wcur(cx.dve, "tabD")
        i = cx.v.tensor_scalar_mul(self.tab[0][:, :], self.tab[0][:, :], scol) if cx.dve else None
        cx.inc("dve", i)
        i = cx.v.tensor_scalar_mul(self.tab[1][:, :], self.tab[1][:, :], ssw) if cx.dve else None
        cx.inc("dve", i)
        for ft in range(4):
            cx.wcur(cx.act, "dve")
            cx.wcur(cx.act, "evQ")
            cx.wcur(cx.act, "pe")
            i = cx.s.activation(self.sqb[0][:, :], self.ust[ft][:, :], AF.Square) if cx.act else None
            cx.inc("act", i)
            cx.wcur(cx.pe, "act")
            if ft == 0:
                cx.wcur(cx.pe, "rnev")
            i = cx.t.matmul(self.ps["msc"][0:8, :], self.c_qones[:, 8 * ft:8 * ft + 8],
                            self.sqb[0][:, :], start=(ft == 0), stop=(ft == 3)) if cx.pe else None
            cx.inc("pe", i)
            cx.wcur(cx.sp, "act")
            for blk in range(4):
                sb_ = (32 * ((blk + 1) if blk % 2 == 0 else (blk - 1)))
                ins = cx.q.dma_start(out=self.rh[sb_:sb_ + 32, :],
                                     in_=self.ust[ft][32 * blk:32 * blk + 32, :]) if cx.sp else None
                cx.inc("rh", ins, 16)
            cx.wcur(cx.dve, "tabD")
            cx.wcur(cx.dve, "rh")
            cx.wcur(cx.dve, "act")
            i = cx.v.tensor_mul(self.ust[ft][:, :], self.ust[ft][:, :], self.tab[0][:, :]) if cx.dve else None
            cx.inc("dve", i)
            i = cx.v.tensor_mul(self.rh[:, :], self.rh[:, :], self.tab[1][:, :]) if cx.dve else None
            cx.inc("dve", i)
            if qk == "k":
                i = cx.v.tensor_add(self.ust[ft][:, :], self.ust[ft][:, :], self.rh[:, :]) if cx.dve else None
                cx.inc("dve", i)
            else:
                i = cx.v.tensor_add(dstslot[:, 512 * ft:512 * ft + 512],
                                    self.ust[ft][:, :], self.rh[:, :]) if cx.dve else None
                cx.inc("dve", i)
        cx.wcur(cx.dve, "pe")
        i = cx.v.tensor_scalar_max(self.rnr[0:8, :], self.ps["msc"][0:8, :], 1e-24) if cx.dve else None
        cx.inc("dve", i)
        cx.wcur(cx.act, "dve")
        i = cx.s.activation(self.rnr[0:8, :], self.rnr[0:8, :], AF.Ln) if cx.act else None
        cx.inc("act", i)
        i = cx.s.activation(self.rnb[0:8, :], self.rnr[0:8, :], AF.Exp, scale=-0.5) if cx.act else None
        cx.inc("act", i)
        if qk == "q":
            mscbf = self.ps["msc"][:, :].bitcast(BF16)
            cx.wcur(cx.pe, "act")
            cx.wcur(cx.pe, "dve")
            last = None
            for j in range(4):
                i = cx.t.transpose(mscbf[:, 8 * j:8 * j + 8],
                                   self.rnb[0:8, 128 * j:128 * j + 128],
                                   self.c_ident[0:8, 0:8]) if cx.pe else None
                last = i if i is not None else last
                cx.inc("pe", i)
            rq = self.rnqT[0 if c == 0 else 1]
            cx.wcur(cx.dve, "pe")
            i = cx.v.tensor_copy(rq[:, :], mscbf[:, 0:32]) if cx.dve else None
            cx.inc("rnev", i)
        else:
            for ft in range(4):
                cx.wcur(cx.pe, "act")
                cx.wcur(cx.pe, "dve")
                i = cx.t.matmul(self.ps["msc"][:, :], self.c_qsel[:, 128 * ft:128 * ft + 128],
                                self.rnb[0:8, :], start=True, stop=True) if cx.pe else None
                cx.inc("pe", i)
                cx.wcur(cx.dve, "pe")
                i = cx.v.tensor_mul(dstslot[:, 512 * ft:512 * ft + 512],
                                    self.ust[ft][:, :], self.ps["msc"][:, :]) if cx.dve else None
                cx.inc("rnev" if ft == 3 else "dve", i)

    def sub_agkv(self, cx, l):
        cx.wcur(cx.sp, "rnev")
        cx.wcur(cx.sp, "dve")
        cx.wcur(cx.sp, "trD")
        kslot = self.kroll[Build.KSLOT[3]]
        vslot = self.vroll[Build.KSLOT[3]]
        for t in range(4):
            ins = cx.q.dma_start(out=self.d_agkv_in[128 * t:128 * t + 128, :],
                                 in_=kslot[:, 512 * t:512 * t + 512]) if cx.sp else None
            cx.inc("agD", ins, 16)
            ins = cx.q.dma_start(out=self.d_agkv_in[512 + 128 * t:512 + 128 * t + 128, :],
                                 in_=vslot[:, 512 * t:512 * t + 512]) if cx.sp else None
            cx.inc("agD", ins, 16)
        cx.wcur(cx.pool, "agD")
        ins = None
        if cx.pool:
            ins = cx.g.collective_compute(
                "AllGather", OP.bypass, replica_groups=[list(range(NCORES))],
                ins=[self.d_agkv_in[:, :].opt()], outs=[self.d_agkv_out[:, :].opt()])
        cx.inc("cc", ins)

    def sub_agkv_read(self, cx, l):
        cx.wcur(cx.sp, "cc")
        kslot = self.kroll[Build.KSLOT[-1]]
        vslot = self.vroll[Build.KSLOT[-1]]
        for half, dst in ((0, kslot), (1, vslot)):
            ins = None
            if cx.sp:
                rk = self.nc.values_load(self.c_rk[0:1, 0:1], engines=[mybir.EngineType.SP])
                srcap = bass.AP(self.d_agkv_out.tensor, (rk * 1024 + 512 * half) * CH,
                                [[CH, 128], [128 * CH, 4], [1, CH]])
                ins = cx.q.dma_start(out=dst[:, :].rearrange("p (t c) -> p t c", c=CH), in_=srcap)
            cx.inc("agD", ins, 16)

    def sub_attn(self, cx, l, c, it_ix, base):
        cx.wcur(cx.pe, "rnev")
        cx.wcur(cx.pe, "dve")
        cx.wcur(cx.pe, "trD")
        if c == 0:
            cx.wcur(cx.pe, "agD")
        for p in range(4):
            cx.wcur(cx.pe, "oe")
            for j in range(4):
                for h2 in range(2):
                    it = it_ix[0]; it_ix[0] += 1
                    self.attn_iter(cx, l, c, p, j, h2, it, base)
            cx.wcur(cx.dve, "av")
            cx.wcur(cx.dve, "pe")
            i = cx.v.tensor_copy(self.och[:, 512 * p:512 * p + 512], self.ps["o2"][:, :]) if cx.dve else None
            cx.inc("oe", i)
        if "och" in self.taps and l == 0:
            cx.wcur(cx.sp, "oe")
            ins = cx.q.dma_start(out=self.d_och_tap[:, 2048 * c:2048 * c + 2048], in_=self.och[:, :]) if cx.sp else None
            cx.inc("do", ins, 16)
            cx.wcur(cx.dve, "do")

    def attn_iter(self, cx, l, c, p, j, h2, it, base):
        qslot = self.qrot[0 if c == 0 else 1]
        rq = self.rnqT[0 if c == 0 else 1]
        sim = self.ps["sim"]
        cx.wabs(cx.pe, "ex", base["ex"] + it)
        pieces = []
        w1p = 512 - 128 * j
        pieces.append((c - 1, 128 * j, w1p, 0))
        if 128 * j > 0:
            pieces.append((c, 0, 128 * j, w1p))
        pieces.append((c, 128 * j, 128, 512))
        last = None
        for (ck, scol, w, pcol) in pieces:
            kslot = self.kroll[Build.KSLOT[ck]]
            ins = cx.t.matmul(sim[:, pcol:pcol + w],
                              qslot[64 * h2:64 * h2 + 64, 512 * p + 128 * j:512 * p + 128 * j + 128],
                              kslot[64 * h2:64 * h2 + 64, 512 * p + scol:512 * p + scol + w],
                              start=True, stop=True) if cx.pe else None
            last = ins if ins is not None else last
        cx.inc("sm", last)
        cx.wcur(cx.dve, "sm")
        if c == 0:
            Wj = 512 - 128 * j
            off = [0, 512, 896, 1152][j]
            i = cx.v.scalar_tensor_tensor(sim[:, 0:Wj], sim[:, 0:Wj], 1.0,
                                          self.c_mlo[:, off:off + Wj], OP.mult, OP.add) if cx.dve else None
            cx.inc("dve", i)
        else:
            i = cx.v.scalar_tensor_tensor(sim[:, 0:128], sim[:, 0:128], 1.0,
                                          self.c_trilo[:, :], OP.mult, OP.add) if cx.dve else None
            cx.inc("dve", i)
        i = cx.v.scalar_tensor_tensor(sim[:, 512:640], sim[:, 512:640], 1.0,
                                      self.c_trihi[:, :], OP.mult, OP.add) if cx.dve else None
        cx.inc("dve", i)
        i = cx.v.reduce_max(self.negmax[:, :], sim[:, :], AX.X, negate=True) if cx.dve else None
        cx.inc("dve", i)
        cx.wcur(cx.dve, "dve")
        i = cx.v.tensor_mul(self.negmax2[:, :], self.negmax[:, :], rq[:, 8 * j + 2 * p + h2:8 * j + 2 * p + h2 + 1]) if cx.dve else None
        cx.inc("mm", i)
        cx.wabs(cx.act, "tr", base["tr"] + it)
        cx.wabs(cx.act, "sc", base["sc"] + it)
        cx.wcur(cx.act, "mm")
        i = cx.s.activation(self.slab[:, :], sim[:, :], AF.Exp,
                            bias=self.negmax2[:, :], scale=rq[:, 8 * j + 2 * p + h2:8 * j + 2 * p + h2 + 1],
                            accum_out=self.srow[:, :]) if cx.act else None
        cx.inc("ex", i)
        cx.wcur(cx.dve, "ex")
        i = cx.v.reciprocal(self.rrow[:, :], self.srow[:, :]) if cx.dve else None
        cx.inc("dve", i)
        cx.wcur(cx.dve, "dve")
        i = cx.v.tensor_scalar_mul(self.slab[:, :], self.slab[:, :], self.rrow[:, :]) if cx.dve else None
        cx.inc("sc", i)
        cx.wabs(cx.pe, "te", base["te"] + it)
        cx.wcur(cx.pe, "sc")
        last = None
        for c2 in range(5):
            ins = cx.t.transpose(self.ps["aT"][:, 128 * c2:128 * c2 + 128],
                                 self.slab[:, 128 * c2:128 * c2 + 128],
                                 self.c_ident[:, :]) if cx.pe else None
            last = ins if ins is not None else last
        cx.inc("tr", last)
        cx.wabs(cx.act, "av", base["av"] + it)
        cx.wcur(cx.act, "tr")
        i = cx.s.copy(self.aTs[:, :], self.ps["aT"][:, :]) if cx.act else None
        cx.inc("te", i)
        cx.wcur(cx.pe, "te")
        last = None
        for c2 in range(5):
            kt = 512 * c + 128 * j - 512 + 128 * c2
            ck = kt // 512
            tt = (kt % 512) // 128
            vslot = self.vroll[Build.KSLOT[ck]]
            ins = cx.t.matmul(self.ps["o2"][64 * h2:64 * h2 + 64, 128 * j:128 * j + 128],
                              vslot[:, 512 * tt + 64 * (2 * p + h2):512 * tt + 64 * (2 * p + h2) + 64],
                              self.aTs[:, 128 * c2:128 * c2 + 128],
                              start=(c2 == 0), stop=(c2 == 4),
                              tile_position=(0, 64 * h2)) if cx.pe else None
            last = ins if ins is not None else last
        cx.inc("av", last)

    def sub_wo(self, cx, l, c, pp):
        col = 1 + CH * c
        woff = l * 512 * D
        evq0 = cx.counts.get("evQ", 0)
        for ft in range(NT):
            slot, li = self.wload(cx, self.d_fwo.tensor, woff + 128 * ft, D, 4, 128)
            cx.wabs(cx.pe, f"dw{slot}", 16 * (li // self.NSLOT + 1))
            if ft >= 2:
                cx.wabs(cx.pe, "evQ", evq0 + ft - 1)
            cx.wcur(cx.pe, "oe")
            pb = pp[ft % 2]
            last = None
            for k in range(4):
                ins = cx.t.matmul(pb[:, :], self.ring[slot][:, 128 * k:128 * k + 128],
                                  self.och[:, 512 * k:512 * k + 512],
                                  start=(k == 0), stop=(k == 3)) if cx.pe else None
                last = ins if ins is not None else last
            cx.inc("pe", last)
            self.ring_mark(cx, li)
            cx.wcur(cx.dve, "pe")
            i = cx.v.scalar_tensor_tensor(self.xf[ft][:, col:col + CH],
                                          self.xf[ft][:, col:col + CH], 1.0,
                                          pb[:, :], OP.mult, OP.add) if cx.dve else None
            cx.inc("evQ", i)

    def ph_B(self, cx, l):
        psB = [self.ps[f"b{i}"] for i in range(4)]
        psW = [self.ps["w20"], self.ps["w21"]]
        cx.wcur(cx.dve, "act")
        for t in range(NT):
            i = cx.v.tensor_copy(self.bst[t][:, :], self.xf[t][:, 0:1537:512]) if cx.dve else None
            cx.inc("dve", i)
        for c in range(NCH):
            self.sub_ln(cx, l, c, psB)
            self.sub_w1(cx, l, c, psB)
            self.sub_w2(cx, l, c, psW)
        if l == 0:
            self.agx_export(cx)

    def sub_ln(self, cx, l, c, psB):
        col = 1 + CH * c
        for t in range(NT):
            if t >= 4:
                cx.wcur(cx.dve, "lnp")
                i = cx.v.tensor_copy(self.h2f[t - 4][:, :], self.xf[t][:, col - 1:col - 1 + CH]) if cx.dve else None
                cx.inc("dve", i)
                cx.wcur(cx.dve, "dve")
                i = cx.v.tensor_copy(self.h2f[t - 4][:, 0:1], self.bst[t][:, c:c + 1]) if cx.dve else None
                cx.inc("dve", i)
                cx.wcur(cx.act, "dve")
                i = cx.s.copy(self.xpb[t][:, :], self.h2f[t - 4][:, :]) if cx.act else None
                cx.inc("act", i)
            else:
                cx.wcur(cx.act, "lnp")
                cx.wcur(cx.act, "pe")
                i = cx.s.copy(self.xpb[t][:, :], self.xf[t][:, col:col + CH]) if cx.act else None
                cx.inc("act", i)
        for t in range(NT):
            src32 = self.h2f[t - 4] if t >= 4 else None
            cx.wcur(cx.act, "pe")
            if src32 is not None:
                i = cx.s.activation(self.sqb[t % 2][:, :], src32[:, :], AF.Square) if cx.act else None
            else:
                i = cx.s.activation(self.sqb[t % 2][:, :], self.xf[t][:, col:col + CH], AF.Square) if cx.act else None
            cx.inc("act", i)
            cx.wcur(cx.pe, "act")
            if t == 0:
                cx.wcur(cx.pe, "lnp")
                cx.wcur(cx.pe, "hpe")
            i = cx.t.matmul(psB[0][0:1, :], self.c_ones128b[:, :], self.xpb[t][:, :],
                            start=(t == 0), stop=(t == NT - 1)) if cx.pe else None
            cx.inc("pe", i)
            i = cx.t.matmul(psB[1][0:1, :], self.c_ones128b[:, :], self.sqb[t % 2][:, :],
                            start=(t == 0), stop=(t == NT - 1)) if cx.pe else None
            cx.inc("pe", i)
        cx.wcur(cx.dve, "pe")
        i = cx.v.tensor_scalar_mul(self.r_mu[:, :], psB[0][0:1, :], 1.0 / D) if cx.dve else None
        cx.inc("dve", i)
        cx.wcur(cx.act, "dve")
        i = cx.s.activation(self.rowa[:, :], self.r_mu[:, :], AF.Square) if cx.act else None
        cx.inc("act", i)
        cx.wcur(cx.dve, "act")
        i = cx.v.scalar_tensor_tensor(self.rowb[:, :], psB[1][0:1, :], 1.0 / D, self.rowa[:, :],
                                      OP.mult, OP.subtract) if cx.dve else None
        cx.inc("dve", i)
        cx.wcur(cx.act, "dve")
        i = cx.s.activation(self.rowa[:, :], self.rowb[:, :], AF.Ln, bias=1e-5) if cx.act else None
        cx.inc("act", i)
        i = cx.s.activation(self.r_rs[:, :], self.rowa[:, :], AF.Exp, scale=-0.5) if cx.act else None
        cx.inc("act", i)
        cx.wcur(cx.pe, "act")
        i = cx.t.matmul(psB[2][:, :], self.c_onesf[:, :], self.r_mu[:, :], start=True, stop=True) if cx.pe else None
        cx.inc("pe", i)
        i = cx.t.matmul(psB[3][:, :], self.c_onesf[:, :], self.r_rs[:, :], start=True, stop=True) if cx.pe else None
        cx.inc("pe", i)
        for t in range(NT):
            src32 = self.h2f[t - 4][:, :] if t >= 4 else self.xf[t][:, col:col + CH]
            cx.wcur(cx.dve, "pe")
            i = cx.v.scalar_tensor_tensor(self.st5[t % 2][:, :], psB[2][:, :], -1.0,
                                          src32, OP.mult, OP.add) if cx.dve else None
            cx.inc("dve", i)
            i = cx.v.tensor_mul(self.st5[t % 2][:, :], self.st5[t % 2][:, :], psB[3][:, :]) if cx.dve else None
            cx.inc("dve", i)
            cx.wcur(cx.act, "dve")
            i = cx.s.activation(self.lnc[t][:, :], self.st5[t % 2][:, :], AF.Identity,
                                scale=self.c_lng[:, 8 * l + t:8 * l + t + 1],
                                bias=self.c_lnb[:, 8 * l + t:8 * l + t + 1]) if cx.act else None
            cx.inc("lnp", i)

    def sub_w1(self, cx, l, c, psB):
        woff1 = l * D * W1W
        hp0 = cx.counts.get("hpe", 0)
        for hp in range(NHP):
            slot, li = self.wload(cx, self.d_fw1.tensor, woff1 + 256 * hp, W1W, NT, 256)
            cx.wabs(cx.pe, f"dw{slot}", 16 * (li // self.NSLOT + 1))
            if hp >= 2:
                cx.wabs(cx.pe, "hpe", hp0 + hp - 1)
            cx.wcur(cx.pe, "lnp")
            pa = psB[2 * (hp % 2)]
            pg = psB[2 * (hp % 2) + 1]
            lasta = lastg = None
            for k in range(NT):
                ins = cx.t.matmul(pa[:, :], self.ring[slot][:, 256 * k:256 * k + 128],
                                  self.lnc[k][:, :], start=(k == 0), stop=(k == NT - 1)) if cx.pe else None
                lasta = ins if ins is not None else lasta
                ins = cx.t.matmul(pg[:, :], self.ring[slot][:, 256 * k + 128:256 * k + 256],
                                  self.lnc[k][:, :], start=(k == 0), stop=(k == NT - 1)) if cx.pe else None
                lastg = ins if ins is not None else lastg
            cx.inc("pe", lastg)
            self.ring_mark(cx, li)
            cx.wcur(cx.act, "pe")
            cx.wcur(cx.act, "dve")
            i = cx.s.activation(self.gl[:, :], pg[:, :], AF.Gelu,
                                bias=self.c_b1[:, 44 * l + 2 * hp + 1:44 * l + 2 * hp + 2]) if cx.act else None
            cx.inc("act", i)
            cx.wcur(cx.dve, "act")
            i = cx.v.scalar_tensor_tensor(self.hid[hp][:, :], pa[:, :],
                                          self.c_b1[:, 44 * l + 2 * hp:44 * l + 2 * hp + 1],
                                          self.gl[:, :], OP.add, OP.mult) if cx.dve else None
            cx.inc("hpe", i)

    def sub_w2(self, cx, l, c, psW):
        col = 1 + CH * c
        woff2 = l * HIDP * D
        w2e0 = cx.counts.get("w2e", 0)
        for ft in range(NT):
            slotA, liA = self.wload(cx, self.d_fw2.tensor, woff2 + 128 * ft, D, 11, 128)
            slotB, liB = self.wload(cx, self.d_fw2.tensor, woff2 + 128 * ft + 1408 * D, D, 11, 128)
            cx.wabs(cx.pe, f"dw{slotA}", 16 * (liA // self.NSLOT + 1))
            cx.wabs(cx.pe, f"dw{slotB}", 16 * (liB // self.NSLOT + 1))
            if ft >= 2:
                cx.wabs(cx.pe, "w2e", w2e0 + ft - 1)
            cx.wcur(cx.pe, "hpe")
            pb = psW[ft % 2]
            last = None
            for hp in range(NHP):
                slot = slotA if hp < 11 else slotB
                kk = hp if hp < 11 else hp - 11
                ins = cx.t.matmul(pb[:, :], self.ring[slot][:, 128 * kk:128 * kk + 128],
                                  self.hid[hp][:, :], start=(hp == 0), stop=(hp == NHP - 1)) if cx.pe else None
                last = ins if ins is not None else last
            cx.inc("pe", last)
            self.ring_mark(cx, liA)
            self.ring_mark(cx, liB)
            cx.wcur(cx.dve, "pe")
            i = cx.v.scalar_tensor_tensor(self.xf[ft][:, col:col + CH],
                                          self.xf[ft][:, col:col + CH],
                                          self.c_b2[:, 8 * l + ft:8 * l + ft + 1],
                                          pb[:, :], OP.add, OP.add) if cx.dve else None
            cx.inc("w2e", i)
            if l == 0:
                cx.wcur(cx.act, "w2e")
                i = cx.s.copy(self.xb[ft][:, col:col + CH], self.xf[ft][:, col:col + CH]) if cx.act else None
                cx.inc("act", i)


def build_program(stages=("p1", "attn", "ffn"), dev_taps=(), ffn_parts=None):
    if ffn_parts is None:
        ffn_parts = [("A", 0), ("B", 0), ("A", 1), ("B", 1)]
    b = Build(dev_taps=dev_taps)
    sb = lambda name, shape, dt: b.es.enter_context(b.nc.sbuf_tensor(name, list(shape), dt))
    b.sqb = [sb(f"sqb{i}", (128, CH), BF16) for i in range(2)]
    b.sqf = [sb(f"sqf{i}", (128, CH), F32) for i in range(2)]
    b.st5 = [sb(f"st5{i}", (128, CH), F32) for i in range(2)]
    b.rowa = sb("rowa", (1, CH), F32)
    b.rowb = sb("rowb", (1, CH), F32)
    b.kred = sb("kred", (128, 1), F32)
    for ch in ("evA", "stD", "trD", "qD", "evZ", "evW", "xD", "ksD", "agD",
               "dw0", "dw1", "dw2", "dw3", "xa0", "xa1", "kvD0", "kvD1"):
        b.sem(ch)

    b.block({}, b.ph_consts)
    if "p1" in stages:
        b.block({"sx": ((1, CH), F32), "sx2": ((1, CH), F32),
                 "mu": ((128, CH), F32), "rs": ((128, CH), F32)}, b.ph_p1)
    with_xacc = "attn" in stages
    if with_xacc:
        with ExitStack() as aes:
            asb = lambda name, shape, dt: aes.enter_context(b.nc.sbuf_tensor(name, list(shape), dt))
            b.ksumc = asb("ksumc", (128, NT), F32)
            b.ksumb = asb("ksumb", (128, NT), BF16)
            b.ksum4 = asb("ksum4", (128, 4 * NT), F32)
            b.ksumt = asb("ksumt", (128, 2 * NT), F32)
            with ExitStack() as a1:
                a1sb = lambda name, shape, dt: a1.enter_context(b.nc.sbuf_tensor(name, list(shape), dt))
                b.qstage = a1sb("qstage", (128, 8 * CH), BF16)
                b.kstage = a1sb("kstage", (128, 8 * CH), BF16)
                b.vstage = a1sb("vstage", (128, 8 * CH), BF16)
                b.block({"qk0": ((128, CH), F32), "qk1": ((128, CH), F32)}, b.ph_p2)
            with ExitStack() as a2:
                a2sb = lambda name, shape, dt: a2.enter_context(b.nc.sbuf_tensor(name, list(shape), dt))
                b.ktm = [a2sb(f"ktm{i}", (128, 512), BF16) for i in range(2)]
                b.vtm = [a2sb(f"vtm{i}", (128, 1024), BF16) for i in range(2)]
                b.kvstage = [a2sb(f"kvstage{i}", (128, 1024), F32) for i in range(2)]
                b.block({f"kv{m}": ((128, 1024), F32) for m in range(4)}, b.ph_p3)
            with ExitStack() as a2b:
                absb = lambda name, shape, dt: a2b.enter_context(b.nc.sbuf_tensor(name, list(shape), dt))
                b.agldt = absb("agldt", (128, NCORES * 1024), F32)
                b.agu = [absb(f"agu{i}", (128, 1024), F32) for i in range(4)]
                b.block({}, b.ph_p3r)
            with ExitStack() as a3:
                a3sb = lambda name, shape, dt: a3.enter_context(b.nc.sbuf_tensor(name, list(shape), dt))
                b.kvb = [a3sb(f"kvb{t}", (128, 1024), BF16) for t in range(NT)]
                b.ksg = a3sb("ksg", (128, NCORES * NT), F32)
                b.ksg2 = a3sb("ksg2", (128, 4 * NT), F32)
                b.qch = [a3sb(f"qch{t}", (128, CH), BF16) for t in range(NT)]
                b.azb = [a3sb(f"azb{t}", (128, CH), BF16) for t in range(NT)]
                b.zbc_sb = a3sb("zbc_sb", (128, CH), F32)
                b.xastage = [a3sb(f"xastage{i}", (128, CH), F32) for i in range(2)]
                if "bounce" in dev_taps:
                    b.ztap_sb = a3sb("ztap_sb", (1, T), F32)
                    b.kstap_sb = a3sb("kstap_sb", (128, NT), F32)
                b.block({"zden": ((1, CH), F32), "zbc": ((128, CH), F32),
                         "at0": ((128, CH), F32), "at1": ((128, CH), F32),
                         "wo0": ((128, CH), F32), "wo1": ((128, CH), F32)}, b.ph_af)
    if "ffn" in stages:
        for ch in ("sm", "mm", "ex", "sc", "tr", "te", "av", "oe", "evQ", "rnev",
                   "rh", "tabD", "vbD", "lnp", "hpe", "w2e"):
            if ch not in b.sems:
                b.sem(ch)
        for l in range(2):
            if ("A", l) in ffn_parts:
              with ExitStack() as fes:
                fsb = lambda name, shape, dt: fes.enter_context(b.nc.sbuf_tensor(f"{name}_l{l}", list(shape), dt))
                b.kroll = [fsb(f"kroll{i}", (128, 2048), BF16) for i in range(4)]
                b.vroll = [fsb(f"vroll{i}", (128, 2048), BF16) for i in range(4)]
                b.qrot = [fsb(f"qrot{i}", (128, 2048), BF16) for i in range(2)]
                b.ust = [fsb(f"ust{i}", (128, CH), F32) for i in range(4)]
                b.rh = fsb("rh", (128, CH), F32)
                b.vfm = fsb("vfm", (128, CH), BF16)
                b.tab = [fsb(f"tab{i}", (128, CH), F32) for i in range(2)]
                b.slab = fsb("slab", (128, 640), BF16)
                b.aTs = fsb("aTs", (128, 640), BF16)
                b.och = fsb("och", (128, 2048), BF16)
                b.rnr = fsb("rnr", (8, CH), F32)
                b.rnb = fsb("rnb", (8, CH), BF16)
                b.rnqT = [fsb(f"rnqT{i}", (128, 32), F32) for i in range(2)]
                b.negmax = fsb("negmax", (128, 1), F32)
                b.negmax2 = fsb("negmax2", (128, 1), F32)
                b.srow = fsb("srow", (128, 1), F32)
                b.rrow = fsb("rrow", (128, 1), F32)
                b.block({"pp0": ((128, CH), F32), "pp1": ((128, CH), F32),
                         "msc": ((128, CH), F32), "sim": ((128, 640), F32),
                         "aT": ((128, 640), BF16), "o2": ((128, CH), F32)},
                        lambda cx, l=l: b.ph_A(cx, l))
            if ("B", l) in ffn_parts:
              with ExitStack() as fes:
                fsb = lambda name, shape, dt: fes.enter_context(b.nc.sbuf_tensor(f"{name}_l{l}", list(shape), dt))
                b.bst = [fsb(f"bst{t}", (128, 4), F32) for t in range(NT)]
                b.h2f = [fsb(f"h2f{t}", (128, CH), F32) for t in range(4)]
                b.xpb = [fsb(f"xpb{t}", (128, CH), BF16) for t in range(NT)]
                b.lnc = [fsb(f"lnc{t}", (128, CH), BF16) for t in range(NT)]
                b.gl = fsb("gl", (128, CH), F32)
                b.hid = [fsb(f"hid{t}", (128, CH), BF16) for t in range(NHP)]
                b.block({"b0": ((128, CH), F32), "b1": ((128, CH), F32),
                         "b2": ((128, CH), F32), "b3": ((128, CH), F32),
                         "w20": ((128, CH), F32), "w21": ((128, CH), F32)},
                        lambda cx, l=l: b.ph_B(cx, l))
    b.block({}, lambda cx: b.ph_finale(cx, with_xacc))
    b.es.close()
    return b


def kernel(**inputs):
    per_core = _host_prep(inputs)
    b = build_program(stages=("p1", "attn", "ffn"))
    res = run_bass_kernel_spmd(b.nc, per_core, core_ids=list(range(NCORES)))
    shards = [np.asarray(r["outt"]) for r in res.results]
    out = np.concatenate([s.T for s in shards], axis=0)[None]
    return out.astype(np.float32)



# version 02 (final)
# speedup vs baseline: 1.1072x; 1.1072x over previous
import numpy as np
import ml_dtypes
from contextlib import ExitStack

import concourse.bass as bass
import concourse.mybir as mybir
from concourse.bass_utils import run_bass_kernel_spmd

F32 = mybir.dt.float32
BF16 = mybir.dt.bfloat16
I32 = mybir.dt.int32
AF = mybir.ActivationFunctionType
OP = mybir.AluOpType
AX = mybir.AxisListType
BF = ml_dtypes.bfloat16

NCORES = 8
N = 16384
T = N // NCORES
D = 1024
NT = D // 128
CH = 512
NCH = T // CH
HEADS = 8
DH = 64
HID = 2730
HIDP = 2816
NHP = HIDP // 128
W1W = 2 * HIDP
XW = T + 4
NEG = -1.0e30

FCONS = (0.0, 1.0, -1.0, -0.5, 1e-5, 1e-6, 1e-24)

ROLES = ("pe", "dve", "act", "pool", "sp")
_R2A = {"pe": "tensor", "dve": "vector", "act": "scalar", "pool": "gpsimd", "sp": "sync"}



def _rotary_tables():
    half = np.arange(0, DH, 2, dtype=np.float32)
    inv_freq = (1.0 / 10000.0 ** (half / DH)).astype(np.float32)
    t = np.arange(N, dtype=np.float32)
    fr = t[:, None] * inv_freq[None, :]
    fr = np.concatenate([fr, fr], axis=-1)
    base = ((half + 0.4 * DH) / (1.4 * DH)).astype(np.float32)
    power = ((t - N // 2) / 256.0).astype(np.float32)
    sc = base[None, :] ** power[:, None]
    sc = np.concatenate([sc, sc], axis=-1)
    cos = np.cos(fr).astype(np.float32)
    sin = np.sin(fr).astype(np.float32)
    sign = np.where(np.arange(64) < 32, -1.0, 1.0).astype(np.float32)
    QK = 8.0
    qcs = (cos * sc * QK).astype(np.float32)
    qss = (sign[None, :] * sin * sc * QK).astype(np.float32)
    kcs = (cos / sc).astype(np.float32)
    kss = (sign[None, :] * sin / sc).astype(np.float32)
    def fm(a):
        a = np.ascontiguousarray(a.T)
        return np.concatenate([a, a], axis=0)
    return fm(qcs), fm(qss), fm(kcs), fm(kss)


def _col_fm(vec):
    v = np.asarray(vec, np.float32).reshape(NT, 128).T
    return np.ascontiguousarray(v)


def _host_prep(ins):
    x = np.asarray(ins["L0_attn_x"], np.float32)[0]
    xT = np.ascontiguousarray(x.T)

    wqkv_a = np.ascontiguousarray(np.asarray(ins["L0_attn_W_qkv"], np.float32).T)
    wout_t = np.ascontiguousarray(np.asarray(ins["L0_attn_W_out"], np.float32).T)
    bout_c = _col_fm(ins["L0_attn_b_out"])

    fwqkv = np.ascontiguousarray(np.asarray(ins["L0_ffn_wqkv"], np.float32))
    fwo = np.ascontiguousarray(np.asarray(ins["L0_ffn_wo"], np.float32))
    w1 = np.asarray(ins["L0_ffn_w1"], np.float32)
    b1 = np.asarray(ins["L0_ffn_b1"], np.float32)
    w2 = np.asarray(ins["L0_ffn_w2"], np.float32)
    w1p = np.zeros((2, D, W1W), np.float32)
    b1p = np.zeros((2, W1W), np.float32)
    a_part = np.zeros((2, D, HIDP), np.float32); a_part[:, :, :HID] = w1[:, :, :HID]
    g_part = np.zeros((2, D, HIDP), np.float32); g_part[:, :, :HID] = w1[:, :, HID:]
    ab = np.zeros((2, HIDP), np.float32); ab[:, :HID] = b1[:, :HID]
    gb = np.zeros((2, HIDP), np.float32); gb[:, :HID] = b1[:, HID:]
    for hp in range(NHP):
        w1p[:, :, 256 * hp:256 * hp + 128] = a_part[:, :, 128 * hp:128 * hp + 128]
        w1p[:, :, 256 * hp + 128:256 * hp + 256] = g_part[:, :, 128 * hp:128 * hp + 128]
        b1p[:, 256 * hp:256 * hp + 128] = ab[:, 128 * hp:128 * hp + 128]
        b1p[:, 256 * hp + 128:256 * hp + 256] = gb[:, 128 * hp:128 * hp + 128]
    w2p = np.zeros((2, HIDP, D), np.float32)
    w2p[:, :HID, :] = w2
    b1c = np.zeros((2, 128, 2 * NHP), np.float32)
    for hp in range(NHP):
        b1c[:, :, 2 * hp] = b1p[:, 256 * hp:256 * hp + 128]
        b1c[:, :, 2 * hp + 1] = b1p[:, 256 * hp + 128:256 * hp + 256]
    b2c = np.stack([_col_fm(np.asarray(ins["L0_ffn_b2"], np.float32)[l]) for l in range(2)])
    lngc = np.stack([_col_fm(np.asarray(ins["L0_ffn_ln_g"], np.float32)[l]) for l in range(2)])
    lnbc = np.stack([_col_fm(np.asarray(ins["L0_ffn_ln_b"], np.float32)[l]) for l in range(2)])
    qs = np.asarray(ins["L0_ffn_q_scale"], np.float32)
    ks = np.asarray(ins["L0_ffn_k_scale"], np.float32)
    qsc = np.concatenate([qs, qs], axis=1).T.copy()
    ksc = np.concatenate([ks, ks], axis=1).T.copy()
    qsw = np.concatenate([qs[:, 32:], qs[:, :32]], axis=1)
    ksw = np.concatenate([ks[:, 32:], ks[:, :32]], axis=1)
    qsc_sw = np.concatenate([qsw, qsw], axis=1).T.copy()
    ksc_sw = np.concatenate([ksw, ksw], axis=1).T.copy()

    qcs, qss, kcs, kss = _rotary_tables()

    ident = np.eye(128, dtype=BF)
    ones128b = np.ones((128, 1), dtype=BF)
    onesf = np.ones((1, 128), np.float32)
    qones = np.zeros((128, 32), dtype=BF)
    for t in range(4):
        for h in (2 * t, 2 * t + 1):
            qones[64 * (h % 2):64 * (h % 2) + 64, 8 * t + h] = 1
    qsel = np.zeros((8, 512), dtype=BF)
    for t in range(4):
        for m in range(128):
            qsel[2 * t + (m >= 64), 128 * t + m] = 1
    iv = np.arange(128)[:, None]; jv = np.arange(128)[None, :]
    tri_lo = np.where(jv < iv, np.float32(NEG), np.float32(0)).astype(np.float32)
    tri_hi = np.where(jv > iv, np.float32(NEG), np.float32(0)).astype(np.float32)

    per_core = []
    shared = dict(
        wqkv_a=wqkv_a, wout_t=wout_t, bout_c=bout_c,
        fwqkv=fwqkv, fwo=fwo, fw1=w1p, fb1=b1c, fw2=w2p, fb2=b2c,
        flng=lngc, flnb=lnbc, fqs=qsc, fks=ksc, fqs_sw=qsc_sw, fks_sw=ksc_sw,
        ident=ident, ones128b=ones128b, onesf=onesf, qones=qones, qsel=qsel,
        tri_lo=tri_lo, tri_hi=tri_hi,
    )
    widths = [512, 384, 256, 128]
    offs = np.cumsum([0] + widths)
    for c in range(NCORES):
        s = c * T
        mlo = np.zeros((128, 1280), np.float32)
        for j in range(4):
            w = widths[j]
            blk = mlo[:, offs[j]:offs[j] + w]
            if c == 0:
                blk[:] = NEG
            else:
                blk[:, :128] = tri_lo
        m = dict(shared)
        m.update(
            xt=np.ascontiguousarray(xT[:, s:s + T]),
            qcs=np.ascontiguousarray(qcs[:, s:s + T]),
            qss=np.ascontiguousarray(qss[:, s:s + T]),
            kcs=np.ascontiguousarray(kcs[:, s:s + T]),
            kss=np.ascontiguousarray(kss[:, s:s + T]),
            mlo=mlo,
            gate=np.full((128, 1), 0.0 if c == 0 else 1.0, np.float32),
            rkprev=np.full((1, 1), max(c - 1, 0), np.int32),
            rkself=np.full((1, 1), c, np.int32),
            fcon=np.tile(np.asarray([FCONS], np.float32), (128, 1)),
        )
        per_core.append(m)
    return per_core



class Cx:

    def __init__(self, b, role, eng, counts):
        self.b = b
        self.role = role
        self.eng = eng
        self.counts = counts
        self.pe = role == "pe"
        self.dve = role == "dve"
        self.act = role == "act"
        self.pool = role == "pool"
        self.sp = role == "sp"
        self.t = eng if self.pe else None
        self.v = eng if self.dve else None
        self.s = eng if self.act else None
        self.g = eng if self.pool else None
        self.q = eng if self.sp else None

    def inc(self, ch, instr, k=1):
        self.counts[ch] = self.counts.get(ch, 0) + k
        if instr is not None:
            instr.then_inc(self.b.sems[ch], k)
        return self.counts[ch]

    def bump(self, ch, k=1):
        self.counts[ch] = self.counts.get(ch, 0) + k
        return self.counts[ch]

    def wcur(self, role_flag, ch):
        if role_flag:
            self.eng.wait_ge(self.b.sems[ch], self.counts.get(ch, 0))

    def wabs(self, role_flag, ch, val):
        if role_flag:
            self.eng.wait_ge(self.b.sems[ch], val)


class Build:
    SERIAL = False

    def sync_pt(self, cx):
        if not Build.SERIAL:
            return
        for ch in ("pe", "dve", "act", "pool", "evA", "stD", "dw", "ds", "dg"):
            if cx.counts.get(ch):
                cx.eng.wait_ge(self.sems[ch], cx.counts[ch])

    def __init__(self, dev_taps=()):
        self.taps = set(dev_taps)
        self.nc = bass.Bass("TRN2", target_bir_lowering=False)
        self.es = ExitStack()
        self.sems = {}
        self.counts = {}
        self._declare_dram()
        self._alloc_persistent()

    def sem(self, name):
        self.sems[name] = self.es.enter_context(self.nc.semaphore(name))

    def _declare_dram(self):
        nc = self.nc
        def di(name, shape, dt=F32):
            setattr(self, "d_" + name, nc.dram_tensor(name, list(shape), dt, kind="ExternalInput").ap())
        di("xt", (D, T)); di("wqkv_a", (D, 3072)); di("wout_t", (D, D)); di("bout_c", (128, 8))
        di("fwqkv", (2, D, 1536)); di("fwo", (2, 512, D)); di("fw1", (2, D, W1W))
        di("fb1", (2, 128, 2 * NHP)); di("fw2", (2, HIDP, D)); di("fb2", (2, 128, 8))
        di("flng", (2, 128, 8)); di("flnb", (2, 128, 8)); di("fqs", (128, 2)); di("fks", (128, 2))
        di("fqs_sw", (128, 2)); di("fks_sw", (128, 2))
        di("qcs", (128, T)); di("qss", (128, T)); di("kcs", (128, T)); di("kss", (128, T))
        di("ident", (128, 128), BF16); di("ones128b", (128, 1), BF16); di("onesf", (1, 128))
        di("qones", (128, 32), BF16); di("qsel", (8, 512), BF16)
        di("tri_lo", (128, 128)); di("tri_hi", (128, 128)); di("mlo", (128, 1280))
        di("gate", (128, 1)); di("rkprev", (1, 1), I32); di("rkself", (1, 1), I32); di("fcon", (128, len(FCONS)))
        self.d_out = nc.dram_tensor("outt", [D, T], F32, kind="ExternalOutput").ap()
        def dd(name, shape, dt):
            kind = "ExternalOutput" if ("bounce" in self.taps and name in (
                "qprime", "kprime_fm", "vprime_fm", "xacc")) else "Internal"
            setattr(self, "d_" + name, nc.dram_tensor(name, list(shape), dt, kind=kind).ap())
        dd("qprime", (D, T), BF16)
        dd("kprime_fm", (D, T), BF16)
        dd("vprime_fm", (D, T), BF16)
        dd("xacc", (D, T), F32)
        dd("agx_in", (D, 1), F32)
        dd("agx_out", (NCORES * D, 1), F32)
        dd("vbf", (512, T), BF16)
        dd("agkv_in", (2 * 512, CH), BF16)
        dd("agkv_out", (NCORES * 2 * 512, CH), BF16)
        dd("kv_in", (D, D), F32)
        dd("kvag1", (NCORES * D, D), F32)
        dd("kv2_in", (128, D), F32)
        dd("kv_out", (D, D), F32)
        dd("ks_in", (128, NT), F32)
        dd("ks_out", (NCORES * 128, NT), F32)
        if "halo" in self.taps:
            self.d_halo_tap = nc.dram_tensor("halo_tap", [NT, 128], F32, kind="ExternalOutput").ap()
        if "xacc" in self.taps:
            self.d_xacc_tap = nc.dram_tensor("xacc_tap", [D, T], F32, kind="ExternalOutput").ap()
        if "bounce" in self.taps:
            self.d_kv_tap = nc.dram_tensor("kv_tap", [D, D], F32, kind="ExternalOutput").ap()
            self.d_az_tap = nc.dram_tensor("az_tap", [D, T], BF16, kind="ExternalOutput").ap()
            self.d_z_tap = nc.dram_tensor("z_tap", [1, T], F32, kind="ExternalOutput").ap()
            self.d_ks_tap = nc.dram_tensor("ks_tap", [128, NT], F32, kind="ExternalOutput").ap()
            self.d_kvb_tap = nc.dram_tensor("kvb_tap", [D, D], BF16, kind="ExternalOutput").ap()
            self.d_kvin_tap = nc.dram_tensor("kvin_tap", [D, D], F32, kind="ExternalOutput").ap()
            self.d_kvin2_tap = nc.dram_tensor("kvin2_tap", [D, D], BF16, kind="ExternalOutput").ap()
            self.d_ksg_tap = nc.dram_tensor("ksg_tap", [128, NCORES * NT], F32, kind="ExternalOutput").ap()
            self.d_ksc_tap = nc.dram_tensor("ksc_tap", [128, NT], F32, kind="ExternalOutput").ap()
            self.d_ks4_tap = nc.dram_tensor("ks4_tap", [128, 4 * NT], F32, kind="ExternalOutput").ap()
        if "och" in self.taps:
            self.d_och_tap = nc.dram_tensor("och_tap", [128, NCH * 2048], BF16, kind="ExternalOutput").ap()
            self.d_kr_tap = nc.dram_tensor("kr_tap", [128, NCH * 2048], BF16, kind="ExternalOutput").ap()
            self.d_vr_tap = nc.dram_tensor("vr_tap", [128, NCH * 2048], BF16, kind="ExternalOutput").ap()
            self.d_qr_tap = nc.dram_tensor("qr_tap", [128, NCH * 2048], BF16, kind="ExternalOutput").ap()
            self.d_rn_tap = nc.dram_tensor("rn_tap", [128, NCH * 32], F32, kind="ExternalOutput").ap()
        for tp in ("p1", "a0", "b0", "a1", "b1"):
            if tp in self.taps:
                setattr(self, "d_tap_" + tp,
                        nc.dram_tensor("tap_" + tp, [D, T], F32, kind="ExternalOutput").ap())

    def _alloc_persistent(self):
        nc, es = self.nc, self.es
        sb = lambda name, shape, dt: es.enter_context(nc.sbuf_tensor(name, list(shape), dt))
        self.xf = [sb(f"xf{t}", (128, XW), F32) for t in range(NT)]
        self.xb = [sb(f"xb{t}", (128, XW), BF16) for t in range(NT)]
        self.NSLOT = 4
        self.ring = [sb(f"ring{i}", (128, 2048), BF16) for i in range(self.NSLOT)]
        self.r_mu = sb("r_mu", (1, CH), F32)
        self.r_rs = sb("r_rs", (1, CH), F32)
        self.r_z = sb("r_z", (1, CH), F32)
        self.c_ident = sb("c_ident", (128, 128), BF16)
        self.c_ones128b = sb("c_ones128b", (128, 1), BF16)
        self.c_onesf = sb("c_onesf", (1, 128), F32)
        self.c_qones = sb("c_qones", (128, 32), BF16)
        self.c_qsel = sb("c_qsel", (8, 512), BF16)
        self.c_trilo = sb("c_trilo", (128, 128), F32)
        self.c_trihi = sb("c_trihi", (128, 128), F32)
        self.c_mlo = sb("c_mlo", (128, 1280), F32)
        self.c_gate = sb("c_gate", (128, 1), F32)
        self.c_rk = sb("c_rk", (1, 1), I32)
        self.agxs = sb("agxs", (128, NT), F32)
        self.c_rs = sb("c_rs", (1, 1), I32)
        self.c_bout = sb("c_bout", (128, 8), F32)
        self.c_b1 = sb("c_b1", (128, 2 * 2 * NHP), F32)
        self.c_b2 = sb("c_b2", (128, 16), F32)
        self.c_lng = sb("c_lng", (128, 16), F32)
        self.c_lnb = sb("c_lnb", (128, 16), F32)
        self.c_qs = sb("c_qs", (128, 2), F32)
        self.c_ks = sb("c_ks", (128, 2), F32)
        self.c_qs_sw = sb("c_qs_sw", (128, 2), F32)
        self.c_ks_sw = sb("c_ks_sw", (128, 2), F32)
        self.c_fcon = sb("c_fcon", (128, len(FCONS)), F32)
        for i, v in enumerate(FCONS):
            nc.const_aps.register(F32, v, self.c_fcon[:, i:i + 1])
        for ch in ("dw", "rf", "ds", "dg", "do", "cc",
                   "pe", "dve", "act", "pool"):
            self.sem(ch)

    def ring_load(self, cx, src_ap, ncols):
        i = cx.counts.get("rn", 0)
        cx.bump("rn")
        slot = i % self.NSLOT
        if i >= self.NSLOT:
            cx.wabs(cx.pool, "rf", i - self.NSLOT + 1)
        ins = cx.g.dma_start(out=self.ring[slot][:, 0:ncols], in_=src_ap) if cx.pool else None
        cx.inc("dw", ins, 16)
        return slot, i

    def run_block(self, *bodies):
        base = dict(self.counts)
        finals = {}
        with self.nc.Block() as block:
            for role in ROLES:
                counts = dict(base)
                deco = getattr(block, _R2A[role])
                def make(role, counts):
                    def f(eng):
                        cx = Cx(self, role, eng, counts)
                        for body in bodies:
                            body(cx)
                    return f
                deco(make(role, counts))
                finals[role] = counts
        ref = finals[ROLES[0]]
        for role in ROLES[1:]:
            assert finals[role] == ref, f"count divergence {role}: " + str(
                {k: (finals[role].get(k), ref.get(k)) for k in set(finals[role]) | set(ref)
                 if finals[role].get(k) != ref.get(k)})
        self.counts = ref

    _psid = 0

    def block(self, psums, *bodies):
        with ExitStack() as pes:
            self.ps = {}
            for name, (shape, dt) in psums.items():
                Build._psid += 1
                self.ps[name] = pes.enter_context(
                    self.nc.psum_tensor(f"ps_{name}_{Build._psid}", list(shape), dt))
            self.run_block(*bodies)

    def ph_consts(self, cx):
        def ld(dst, src):
            ins = cx.q.dma_start(out=dst, in_=src) if cx.sp else None
            cx.inc("ds", ins, 16)
        for t in range(NT):
            ld(self.xf[t][:, 1:1 + T], self.d_xt[128 * t:128 * t + 128, :])
        ld(self.c_ident[:, :], self.d_ident[:, :])
        ld(self.c_ones128b[:, :], self.d_ones128b[:, :])
        ld(self.c_onesf[:, :], self.d_onesf[:, :])
        ld(self.c_qones[:, :], self.d_qones[:, :])
        ld(self.c_qsel[:, :], self.d_qsel[:, :])
        ld(self.c_trilo[:, :], self.d_tri_lo[:, :])
        ld(self.c_trihi[:, :], self.d_tri_hi[:, :])
        ld(self.c_mlo[:, :], self.d_mlo[:, :])
        ld(self.c_gate[:, :], self.d_gate[:, :])
        ld(self.c_rk[:, :], self.d_rkprev[:, :])
        ld(self.c_rs[:, :], self.d_rkself[:, :])
        ld(self.c_bout[:, :], self.d_bout_c[:, :])
        for l in range(2):
            ld(self.c_b1[:, 44 * l:44 * l + 44], self.d_fb1[l, :, :])
            ld(self.c_b2[:, 8 * l:8 * l + 8], self.d_fb2[l, :, :])
            ld(self.c_lng[:, 8 * l:8 * l + 8], self.d_flng[l, :, :])
            ld(self.c_lnb[:, 8 * l:8 * l + 8], self.d_flnb[l, :, :])
        ld(self.c_qs[:, :], self.d_fqs[:, :])
        ld(self.c_ks[:, :], self.d_fks[:, :])
        ld(self.c_qs_sw[:, :], self.d_fqs_sw[:, :])
        ld(self.c_ks_sw[:, :], self.d_fks_sw[:, :])
        ld(self.c_fcon[:, :], self.d_fcon[:, :])
        for t in range(NT):
            i1 = cx.v.memset(self.xf[t][:, 0:1], 0.0) if cx.dve else None
            cx.inc("dve", i1)
            i2 = cx.v.memset(self.xb[t][:, 0:1], 0.0) if cx.dve else None
            cx.inc("dve", i2)
        if cx.sp:
            cx.eng.wait_ge(self.b_sem("ds"), cx.counts["ds"])

    def b_sem(self, name):
        return self.sems[name]

    def ph_p1(self, cx):
        ps = self.ps
        nds = 16 * (NT)
        for c in range(NCH):
            col = 1 + CH * c
            for t in range(NT):
                sq = self.sqb[t % 2]
                cx.wcur(cx.act, "ds")
                if c * NT + t >= 2:
                    cx.wcur(cx.act, "pe")
                i = cx.s.copy(self.xb[t][:, col:col + CH], self.xf[t][:, col:col + CH]) if cx.act else None
                cx.inc("act", i)
                i = cx.s.activation(sq[:, :], self.xf[t][:, col:col + CH], AF.Square) if cx.act else None
                cx.inc("act", i)
                cx.wcur(cx.pe, "act")
                i = cx.t.matmul(ps["sx"][:, :], self.c_ones128b[:, :],
                                self.xb[t][:, col:col + CH], start=(t == 0), stop=(t == NT - 1)) if cx.pe else None
                cx.inc("pe", i)
                i = cx.t.matmul(ps["sx2"][:, :], self.c_ones128b[:, :],
                                sq[:, :], start=(t == 0), stop=(t == NT - 1)) if cx.pe else None
                cx.inc("pe", i)
            cx.wcur(cx.dve, "pe")
            i = cx.v.tensor_scalar_mul(self.r_mu[:, :], ps["sx"][:, :], 1.0 / D) if cx.dve else None
            cx.inc("dve", i)
            cx.wcur(cx.act, "dve")
            i = cx.s.activation(self.rowa[:, :], self.r_mu[:, :], AF.Square) if cx.act else None
            cx.inc("act", i)
            cx.wcur(cx.dve, "act")
            i = cx.v.scalar_tensor_tensor(self.rowb[:, :], ps["sx2"][:, :], 1.0 / D, self.rowa[:, :],
                                          OP.mult, OP.subtract) if cx.dve else None
            cx.inc("dve", i)
            cx.wcur(cx.act, "dve")
            i = cx.s.activation(self.rowa[:, :], self.rowb[:, :], AF.Ln, bias=1e-5) if cx.act else None
            cx.inc("act", i)
            cx.wcur(cx.act, "act")
            i = cx.s.activation(self.r_rs[:, :], self.rowa[:, :], AF.Exp, scale=-0.5) if cx.act else None
            cx.inc("act", i)
            cx.wcur(cx.pe, "act")
            i = cx.t.matmul(ps["mu"][:, :], self.c_onesf[:, :], self.r_mu[:, :],
                            start=True, stop=True) if cx.pe else None
            cx.inc("pe", i)
            i = cx.t.matmul(ps["rs"][:, :], self.c_onesf[:, :], self.r_rs[:, :],
                            start=True, stop=True) if cx.pe else None
            cx.inc("pe", i)
            for t in range(NT):
                cx.wcur(cx.dve, "pe")
                i = cx.v.scalar_tensor_tensor(self.st5[t % 2][:, :], ps["mu"][:, :], -1.0,
                                              self.xf[t][:, col:col + CH], OP.mult, OP.add) if cx.dve else None
                cx.inc("dve", i)
                i = cx.v.tensor_mul(self.xf[t][:, col:col + CH], self.st5[t % 2][:, :], ps["rs"][:, :]) if cx.dve else None
                cx.inc("dve", i)
                cx.wcur(cx.act, "dve")
                i = cx.s.copy(self.xb[t][:, col:col + CH], self.xf[t][:, col:col + CH]) if cx.act else None
                cx.inc("act", i)
            cx.wcur(cx.pe, "act")
            cx.wcur(cx.pe, "dve")
        self._tap(cx, "p1")

    def _tap(self, cx, name):
        if name not in self.taps:
            return
        d = getattr(self, "d_tap_" + name)
        cx.wcur(cx.sp, "dve")
        cx.wcur(cx.sp, "act")
        cx.wcur(cx.sp, "pe")
        for t in range(NT):
            i = cx.q.dma_start(out=d[128 * t:128 * t + 128, :], in_=self.xf[t][:, 1:1 + T]) if cx.sp else None
            cx.inc("do", i, 16)
        cx.wcur(cx.sp, "do")

    def ph_finale(self, cx, with_xacc):
        do0 = cx.counts.get("do", 0)
        for c in range(NCH):
            col = 1 + CH * c
            for t in range(NT):
                if with_xacc:
                    it = c * NT + t
                    xa = self.sqf[it % 2]
                    if it >= 2:
                        cx.wcur(cx.sp, "dve")
                    ins = cx.q.dma_start(out=xa[:, :], in_=self.d_xacc[128 * t:128 * t + 128, CH * c:CH * c + CH]) if cx.sp else None
                    cx.inc("ds", ins, 16)
                    if it >= 2:
                        cx.wcur(cx.dve, "do")
                    cx.wcur(cx.dve, "ds")
                    i = cx.v.tensor_add(self.st5[it % 2][:, :], self.xf[t][:, col:col + CH], xa[:, :]) if cx.dve else None
                    cx.inc("dve", i)
                    cx.wcur(cx.sp, "dve")
                    ins = cx.q.dma_start(out=self.d_out[128 * t:128 * t + 128, CH * c:CH * c + CH], in_=self.st5[it % 2][:, :]) if cx.sp else None
                    cx.inc("do", ins, 16)
                else:
                    i = cx.q.dma_start(out=self.d_out[128 * t:128 * t + 128, CH * c:CH * c + CH],
                                       in_=self.xf[t][:, col:col + CH]) if cx.sp else None
                    cx.inc("do", i, 16)
        cx.wcur(cx.sp, "do")


    def wload(self, cx, tensor, elem_off, row_stride, nk, ncols):
        i = cx.counts.get("rn", 0)
        cx.bump("rn")
        slot = i % self.NSLOT
        if i >= self.NSLOT:
            cx.wabs(cx.pool, "pe", cx.counts[f"rpe{i - self.NSLOT}"])
        ins = None
        if cx.pool:
            src = bass.AP(tensor, elem_off,
                          [[row_stride, 128], [128 * row_stride, nk], [1, ncols]])
            ins = cx.g.dma_start(out=self.ring[slot][:, 0:nk * ncols], in_=src)
        cx.inc(f"dw{slot}", ins, 16)
        return slot, i

    def ring_mark(self, cx, li):
        cx.counts[f"rpe{li}"] = cx.counts.get("pe", 0)


    NO_AGX = False

    def ph_p2(self, cx):
        if not Build.NO_AGX:
            self.agx_export(cx)
        qk = [self.ps["qk0"], self.ps["qk1"]]
        for c in range(NCH):
            col = 1 + CH * c
            evA0 = cx.counts.get("evA", 0)
            stD_now = cx.counts.get("stD", 0)
            for mt in range(24):
                self.sync_pt(cx)
                slot, li = self.wload(cx, self.d_wqkv_a.tensor, 128 * mt, 3072, NT, 128)
                if mt >= 2:
                    cx.wabs(cx.pe, "evA", evA0 + mt - 1)
                elif c > 0:
                    cx.wabs(cx.pe, "evA", evA0 - 1 + mt)
                cx.wabs(cx.pe, f"dw{slot}", 16 * (li // self.NSLOT + 1))
                pb = qk[mt % 2]
                last = None
                for k in range(NT):
                    ins = cx.t.matmul(pb[:, :], self.ring[slot][:, 128 * k:128 * k + 128],
                                      self.xb[k][:, col:col + CH],
                                      start=(k == 0), stop=(k == NT - 1)) if cx.pe else None
                    last = ins or last
                cx.inc("pe", last)
                self.ring_mark(cx, li)
                if mt < 16:
                    stage = self.qstage if mt < 8 else self.kstage
                    off = 512 * (mt % 8)
                    cx.wcur(cx.act, "pe")
                    cx.wcur(cx.act, "dve")
                    if c > 0:
                        cx.wabs(cx.act, "stD", stD_now)
                    i2 = cx.s.activation(self.st5[mt % 2][:, :], pb[:, :], AF.Relu) if cx.act else None
                    cx.inc("evA", i2)
                    cx.wcur(cx.dve, "evA")
                    i3 = cx.v.tensor_scalar_add(stage[:, off:off + CH], self.st5[mt % 2][:, :], 1e-6) if cx.dve else None
                    cx.inc("dve", i3)
                    if 8 <= mt:
                        kt = mt - 8
                        i4 = cx.v.reduce_sum(self.kred[:, :], self.st5[mt % 2][:, :], AX.X) if cx.dve else None
                        cx.inc("dve", i4)
                        cx.wcur(cx.dve, "dve")
                        i5 = cx.v.tensor_copy(self.ksum4[:, 8 * c + kt:8 * c + kt + 1], self.kred[:, :]) if cx.dve else None
                        cx.inc("dve", i5)
                else:
                    off = 512 * (mt - 16)
                    cx.wcur(cx.dve, "pe")
                    if c > 0:
                        cx.wabs(cx.dve, "stD", stD_now)
                    i2 = cx.v.tensor_copy(self.vstage[:, off:off + CH], pb[:, :]) if cx.dve else None
                    cx.inc("evA", i2)
            cx.wcur(cx.sp, "dve")
            cx.wcur(cx.sp, "evA")
            for stage, dram in ((self.qstage, self.d_qprime), (self.kstage, self.d_kprime_fm),
                                (self.vstage, self.d_vprime_fm)):
                ins = None
                if cx.sp:
                    dst = bass.AP(dram.tensor, CH * c, [[T, 128], [128 * T, NT], [1, CH]])
                    ins = cx.q.dma_start(out=dst, in_=stage[:, :])
                cx.inc("stD", ins, 16)
        cx.wcur(cx.dve, "dve")
        i = cx.v.tensor_add(self.ksumt[:, 0:8], self.ksum4[:, 0:8], self.ksum4[:, 8:16]) if cx.dve else None
        cx.inc("dve", i)
        i = cx.v.tensor_add(self.ksumt[:, 8:16], self.ksum4[:, 16:24], self.ksum4[:, 24:32]) if cx.dve else None
        cx.inc("dve", i)
        cx.wcur(cx.dve, "dve")
        i = cx.v.tensor_add(self.ksumc[:, :], self.ksumt[:, 0:8], self.ksumt[:, 8:16]) if cx.dve else None
        cx.inc("dve", i)
        cx.wcur(cx.dve, "dve")
        i = cx.v.tensor_scalar_add(self.ksumc[:, :], self.ksumc[:, :], T * 1e-6) if cx.dve else None
        cx.inc("dve", i)
        cx.wcur(cx.dve, "dve")
        cx.wcur(cx.sp, "dve")
        ins = cx.q.dma_start(out=self.d_ks_in[:, :], in_=self.ksumc[:, :]) if cx.sp else None
        cx.inc("ksD", ins, 16)
        if "bounce" in self.taps:
            ins = cx.q.dma_start(out=self.d_ksc_tap[:, :], in_=self.ksumc[:, :]) if cx.sp else None
            cx.inc("ksD", ins, 16)
            ins = cx.q.dma_start(out=self.d_ks4_tap[:, :], in_=self.ksum4[:, :]) if cx.sp else None
            cx.inc("ksD", ins, 16)
        cx.wcur(cx.sp, "stD")

    def agx_export(self, cx):
        cx.wcur(cx.sp, "dve")
        cx.wcur(cx.sp, "act")
        for t in range(NT):
            ins = cx.q.dma_start(out=self.d_agx_in[128 * t:128 * t + 128, 0:1],
                                 in_=self.xf[t][:, T:T + 1]) if cx.sp else None
            cx.inc("ds", ins, 16)
        cx.wcur(cx.pool, "ds")
        ins = None
        if cx.pool:
            ins = cx.g.collective_compute(
                "AllGather", OP.bypass, replica_groups=[list(range(NCORES))],
                ins=[self.d_agx_in[:, :].opt()], outs=[self.d_agx_out[:, :].opt()])
        cx.inc("cc", ins)
        cx.wcur(cx.sp, "cc")
        ins = None
        if cx.sp:
            rk = self.nc.values_load(self.c_rk[0:1, 0:1], engines=[mybir.EngineType.SP])
            srcap = bass.AP(self.d_agx_out.tensor, rk * D, [[1, 128], [128, NT], [1, 1]])
            ins = cx.q.dma_start(out=self.agxs[:, :].rearrange("p (t c) -> p t c", c=1),
                                 in_=srcap, allow_slow_non_contiguous=True)
        cx.inc("ds", ins, 16)
        cx.wcur(cx.dve, "ds")
        i = cx.v.tensor_mul(self.agxs[:, :], self.agxs[:, :], self.c_gate[:, 0:1].broadcast_to([128, NT])) if cx.dve else None
        cx.inc("dve", i)
        cx.wcur(cx.dve, "dve")
        for t in range(NT):
            i = cx.v.tensor_copy(self.xf[t][:, 0:1], self.agxs[:, t:t + 1]) if cx.dve else None
            cx.inc("dve", i)
        cx.wcur(cx.act, "dve")
        for t in range(NT):
            i = cx.s.copy(self.xb[t][:, 0:1], self.xf[t][:, 0:1]) if cx.act else None
            cx.inc("act", i)
        if "halo" in self.taps:
            cx.wcur(cx.sp, "act")
            for t in range(NT):
                ins = cx.q.dma_start(out=self.d_halo_tap[t:t + 1, :].rearrange("a (b c) -> (a b) c", b=128), in_=self.xf[t][:, 0:1]) if cx.sp else None
                cx.inc("do", ins, 16)
            cx.wcur(cx.sp, "do")

    def ph_p3(self, cx):
        kv = [self.ps[f"kv{m}"] for m in range(4)]
        self._kvD0 = cx.counts.get("ds", 0)
        for h in range(2):
            pe0 = cx.counts.get("pe", 0)
            for tt in range(16):
                sl = tt % 2
                if tt >= 2:
                    cx.wabs(cx.sp, "pe", pe0 + tt - 1)
                elif h == 1:
                    cx.wabs(cx.sp, "pe", pe0)
                ins = None
                if cx.sp:
                    ins = cx.q.dma_start_transpose(
                        self.ktm[sl][:, :],
                        self.d_kprime_fm[512 * h:512 * h + 512, 128 * tt:128 * tt + 128])
                cx.inc("trD", ins, 16)
                ins = None
                if cx.sp:
                    ins = cx.q.dma_start_transpose(
                        self.vtm[sl][:, :],
                        self.d_vprime_fm[:, 128 * tt:128 * tt + 128])
                cx.inc("trD", ins, 16)
                cx.wcur(cx.pe, "trD")
                if h == 1 and tt == 0:
                    cx.wcur(cx.pe, "kvD0")
                    cx.wcur(cx.pe, "kvD1")
                last = None
                for m in range(4):
                    for nh in range(2):
                        ins = cx.t.matmul(kv[m][:, 512 * nh:512 * nh + 512],
                                          self.ktm[sl][:, 128 * m:128 * m + 128],
                                          self.vtm[sl][:, 512 * nh:512 * nh + 512],
                                          start=(tt == 0), stop=(tt == 15)) if cx.pe else None
                        last = ins or last
                cx.inc("pe", last)
            cx.wcur(cx.dve, "pe")
            nD = 32 if "bounce" in self.taps else 16
            for m in range(4):
                j = 4 * h + m
                if j >= 2:
                    cx.wabs(cx.dve, f"kvD{j % 2}", nD * (j // 2))
                i = cx.v.tensor_copy(self.kvstage[m % 2][:, :], kv[m][:, :]) if cx.dve else None
                cx.inc("dve", i)
                cx.wcur(cx.sp, "dve")
                ins = cx.q.dma_start(out=self.d_kv_in[512 * h + 128 * m:512 * h + 128 * m + 128, 0:D],
                                     in_=self.kvstage[m % 2][:, :]) if cx.sp else None
                cx.inc(f"kvD{j % 2}", ins, 16)
                if "bounce" in self.taps:
                    ins = cx.q.dma_start(out=self.d_kvin_tap[512 * h + 128 * m:512 * h + 128 * m + 128, :],
                                         in_=self.kvstage[m % 2][:, :]) if cx.sp else None
                    cx.inc(f"kvD{j % 2}", ins, 16)
        cx.wcur(cx.pool, "kvD0")
        cx.wcur(cx.pool, "kvD1")
        cx.wcur(cx.pool, "ksD")
        ins = None
        if cx.pool:
            ins = cx.g.collective_compute(
                "AllGather", OP.bypass, replica_groups=[list(range(NCORES))],
                ins=[self.d_ks_in[:, :].opt()], outs=[self.d_ks_out[:, :].opt()])
        cx.inc("cc", ins)
        ins = None
        if cx.pool:
            ins = cx.g.collective_compute(
                "AllGather", OP.bypass, replica_groups=[list(range(NCORES))],
                ins=[self.d_kv_in[:, :].opt()], outs=[self.d_kvag1[:, :].opt()])
        cx.inc("cc", ins)

    def ph_p3r(self, cx):
        cx.wcur(cx.sp, "cc")
        ins = None
        if cx.sp:
            rs = self.nc.values_load(self.c_rs[0:1, 0:1], engines=[mybir.EngineType.SP])
            srcap = bass.AP(self.d_kvag1.tensor, (rs * 128) * D,
                            [[D, 128], [D * D, NCORES], [1, D]])
            ins = cx.q.dma_start(out=self.agldt[:, :].rearrange("p (r c) -> p r c", c=D), in_=srcap)
        cx.inc("agD", ins, 16)
        cx.wcur(cx.dve, "agD")
        L = lambda r: self.agldt[:, D * r:D * r + D]
        pairs = [(self.agu[0], L(0), L(1)),
                 (self.agu[1], L(2), L(3)),
                 (self.agu[2], L(4), L(5)),
                 (self.agu[3], L(6), L(7)),
                 (L(0), self.agu[0], self.agu[1]),
                 (L(1), self.agu[2], self.agu[3]),
                 (self.agu[0], L(0), L(1))]
        for (o, a, bb) in pairs:
            i = cx.v.tensor_add(o[:, :], a[:, :], bb[:, :]) if cx.dve else None
            cx.inc("dve", i)
        cx.wcur(cx.sp, "dve")
        ins = cx.q.dma_start(out=self.d_kv2_in[:, :], in_=self.agu[0][:, :]) if cx.sp else None
        cx.inc("agD", ins, 16)
        cx.wcur(cx.pool, "agD")
        ins = None
        if cx.pool:
            ins = cx.g.collective_compute(
                "AllGather", OP.bypass, replica_groups=[list(range(NCORES))],
                ins=[self.d_kv2_in[:, :].opt()], outs=[self.d_kv_out[:, :].opt()])
        cx.inc("cc", ins)

    DELAY_AFTER_CC = False

    def ph_af(self, cx):
        cx.wcur(cx.pool, "cc")
        cx.wcur(cx.sp, "cc")
        if Build.DELAY_AFTER_CC:
            for dd in range(8):
                ins = cx.g.dma_start(out=self.d_qprime[:, 0:T], in_=self.d_qprime[:, 0:T]) if cx.pool else None
                cx.inc("dg", ins, 16)
                cx.wcur(cx.pool, "dg")
        if "bounce" in self.taps:
            ins = cx.g.dma_start(out=self.d_kv_tap[:, :], in_=self.d_kv_out[:, :]) if cx.pool else None
            cx.inc("dg", ins, 16)
        for t in range(NT):
            ins = cx.g.dma_start(out=self.kvb[t][:, :],
                                 in_=self.d_kv_out[128 * t:128 * t + 128, :]) if cx.pool else None
            cx.inc("dg", ins, 16)
        for r in range(NCORES):
            ins = cx.q.dma_start(out=self.ksg[:, NT * r:NT * r + NT],
                                 in_=self.d_ks_out[128 * r:128 * r + 128, :]) if cx.sp else None
            cx.inc("qD", ins, 16)
        cx.wcur(cx.dve, "qD")
        if "bounce" in self.taps:
            cx.wcur(cx.sp, "qD")
            ins = cx.q.dma_start(out=self.d_ksg_tap[:, :], in_=self.ksg[:, :]) if cx.sp else None
            cx.inc("do", ins, 16)
        for st, (a, bb) in enumerate(((0, 1), (2, 3), (4, 5), (6, 7))):
            i = cx.v.tensor_add(self.ksg2[:, NT * st:NT * st + NT],
                                self.ksg[:, NT * a:NT * a + NT],
                                self.ksg[:, NT * bb:NT * bb + NT]) if cx.dve else None
            cx.inc("dve", i)
        cx.wcur(cx.dve, "dve")
        i = cx.v.tensor_add(self.ksg[:, 0:NT], self.ksg2[:, 0:NT], self.ksg2[:, NT:2 * NT]) if cx.dve else None
        cx.inc("dve", i)
        i = cx.v.tensor_add(self.ksg[:, NT:2 * NT], self.ksg2[:, 2 * NT:3 * NT], self.ksg2[:, 3 * NT:4 * NT]) if cx.dve else None
        cx.inc("dve", i)
        cx.wcur(cx.dve, "dve")
        i = cx.v.tensor_add(self.ksumb[:, :], self.ksg[:, 0:NT], self.ksg[:, NT:2 * NT]) if cx.dve else None
        cx.inc("dve", i)
        cx.wcur(cx.dve, "dve")
        if "bounce" in self.taps:
            i = cx.v.tensor_copy(self.kstap_sb[:, :], self.ksumb[:, :]) if cx.dve else None
            cx.inc("dve", i)
            cx.wcur(cx.sp, "dve")
            ins = cx.q.dma_start(out=self.d_ks_tap[:, :], in_=self.kstap_sb[:, :]) if cx.sp else None
            cx.inc("do", ins, 16)
        at = [self.ps["at0"], self.ps["at1"]]
        wo = [self.ps["wo0"], self.ps["wo1"]]
        for c in range(NCH):
            cx.counts["afc"] = c
            cx.wcur(cx.sp, "pe")
            for t in range(NT):
                ins = None
                if cx.sp:
                    ins = cx.q.dma_start(out=self.qch[t][:, :],
                                         in_=self.d_qprime[128 * t:128 * t + 128, CH * c:CH * c + CH])
                cx.inc("qD", ins, 16)
            cx.wcur(cx.pe, "qD")
            cx.wcur(cx.pe, "dg")
            last = None
            for k in range(NT):
                ins = cx.t.matmul(self.ps["zden"][:, :], self.ksumb[:, k:k + 1],
                                  self.qch[k][:, :], start=(k == 0), stop=(k == NT - 1)) if cx.pe else None
                last = ins or last
            cx.inc("pe", last)
            cx.wcur(cx.dve, "pe")
            i = cx.v.tensor_scalar_add(self.rowa[:, :], self.ps["zden"][:, :], 1e-6) if cx.dve else None
            cx.inc("dve", i)
            i = cx.v.reciprocal(self.r_z[:, :], self.rowa[:, :]) if cx.dve else None
            cx.inc("dve", i)
            cx.wcur(cx.pe, "dve")
            i = cx.t.matmul(self.ps["zbc"][:, :], self.c_onesf[:, :], self.r_z[:, :],
                            start=True, stop=True) if cx.pe else None
            cx.inc("pe", i)
            cx.wcur(cx.dve, "pe")
            i = cx.v.tensor_copy(self.zbc_sb[:, :], self.ps["zbc"][:, :]) if cx.dve else None
            cx.inc("dve", i)
            cx.wcur(cx.pe, "dg")
            evZ0 = cx.counts.get("evZ", 0)
            for et in range(NT):
                if et >= 2:
                    cx.wabs(cx.pe, "evZ", evZ0 + et - 1)
                pb = at[et % 2]
                last = None
                for k in range(NT):
                    ins = cx.t.matmul(pb[:, :], self.kvb[k][:, 128 * et:128 * et + 128],
                                      self.qch[k][:, :], start=(k == 0), stop=(k == NT - 1)) if cx.pe else None
                    last = ins or last
                cx.inc("pe", last)
                cx.wcur(cx.dve, "pe")
                i = cx.v.scalar_tensor_tensor(self.azb[et][:, :], pb[:, :], 1.0,
                                              self.zbc_sb[:, :], OP.mult, OP.mult) if cx.dve else None
                cx.inc("evZ", i)
            if "bounce" in self.taps:
                cx.wcur(cx.dve, "dve")
                i = cx.v.tensor_copy(self.ztap_sb[:, CH * c:CH * c + CH], self.r_z[:, :]) if cx.dve else None
                cx.inc("dve", i)
            evW0 = cx.counts.get("evW", 0)
            for ft in range(NT):
                slot, li = self.wload(cx, self.d_wout_t.tensor, 128 * ft, D, NT, 128)
                cx.wabs(cx.pe, f"dw{slot}", 16 * (li // self.NSLOT + 1))
                if ft >= 2:
                    cx.wabs(cx.pe, "evW", evW0 + ft - 1)
                cx.wcur(cx.pe, "evZ")
                pb = wo[ft % 2]
                last = None
                for k in range(NT):
                    ins = cx.t.matmul(pb[:, :], self.ring[slot][:, 128 * k:128 * k + 128],
                                      self.azb[k][:, :], start=(k == 0), stop=(k == NT - 1)) if cx.pe else None
                    last = ins or last
                cx.inc("pe", last)
                self.ring_mark(cx, li)
                if ft >= 2:
                    cx.wabs(cx.sp, "evW", evW0 + ft - 1)
                ins = None
                if cx.sp:
                    ins = cx.q.dma_start(out=self.sqf[ft % 2][:, :],
                                         in_=self.d_xt[128 * ft:128 * ft + 128, CH * c:CH * c + CH])
                cx.inc("xD", ins, 16)
                if ft >= 2:
                    cx.wabs(cx.dve, f"xa{ft % 2}", 16 * (cx.counts.get("afc", 0) * 4 + ft // 2))
                cx.wcur(cx.dve, "pe")
                cx.wcur(cx.dve, "xD")
                i = cx.v.scalar_tensor_tensor(self.xastage[ft % 2][:, :], pb[:, :],
                                              self.c_bout[:, ft:ft + 1], self.sqf[ft % 2][:, :],
                                              OP.add, OP.add) if cx.dve else None
                cx.inc("evW", i)
                cx.wcur(cx.sp, "evW")
                ins = cx.q.dma_start(out=self.d_xacc[128 * ft:128 * ft + 128, CH * c:CH * c + CH],
                                     in_=self.xastage[ft % 2][:, :]) if cx.sp else None
                cx.inc(f"xa{ft % 2}", ins, 16)
        cx.wabs(cx.sp, "xa0", 16 * NCH * 4)
        cx.wabs(cx.sp, "xa1", 16 * NCH * 4)
        if "bounce" in self.taps:
            for t in range(NT):
                ins = cx.g.dma_start(out=self.kvb[t][:, :], in_=self.d_kv_in[128 * t:128 * t + 128, :]) if cx.pool else None
                cx.inc("dg", ins, 16)
            cx.wcur(cx.sp, "dg")
            for t in range(NT):
                ins = cx.q.dma_start(out=self.d_kvin2_tap[128 * t:128 * t + 128, :],
                                     in_=self.kvb[t][:, :]) if cx.sp else None
                cx.inc("do", ins, 16)
            cx.wcur(cx.sp, "do")
            cx.wcur(cx.pool, "do")
            for t in range(NT):
                ins = cx.g.dma_start(out=self.kvb[t][:, :], in_=self.d_kv_out[128 * t:128 * t + 128, :]) if cx.pool else None
                cx.inc("dg", ins, 16)
            cx.wcur(cx.sp, "dg")
            cx.wcur(cx.sp, "dve")
            ins = cx.q.dma_start(out=self.d_z_tap[:, :], in_=self.ztap_sb[:, :]) if cx.sp else None
            cx.inc("do", ins, 16)
            for t in range(NT):
                ins = cx.q.dma_start(out=self.d_kvb_tap[128 * t:128 * t + 128, :],
                                     in_=self.kvb[t][:, :]) if cx.sp else None
                cx.inc("do", ins, 16)
            cx.wcur(cx.sp, "do")
        if "xacc" in self.taps:
            for t in range(NT):
                ins = cx.q.dma_start(out=self.d_xacc_tap[128 * t:128 * t + 128, :],
                                     in_=self.d_xacc[128 * t:128 * t + 128, :]) if cx.sp else None
                cx.inc("do", ins, 16)
            cx.wcur(cx.sp, "do")


    KSLOT = {-1: 0, 0: 1, 1: 2, 2: 3, 3: 2}

    A_PARTS = ("attn", "wo")

    def ph_A(self, cx, l):
        pp = [self.ps["pp0"], self.ps["pp1"]]
        it_ix = [0]
        base = {ch: cx.counts.get(ch, 0) for ch in ("ex", "tr", "te", "av", "sc", "oe")}
        at = "attn" in Build.A_PARTS
        wo = "wo" in Build.A_PARTS

        self.sub_qkvrot(cx, l, 0, pp)
        self.sub_qkvrot(cx, l, 1, pp)
        if at: self.sub_attn(cx, l, 1, it_ix, base)
        if wo: self.sub_wo(cx, l, 1, pp)
        self.sub_qkvrot(cx, l, 2, pp)
        if at: self.sub_attn(cx, l, 2, it_ix, base)
        if wo: self.sub_wo(cx, l, 2, pp)
        self.sub_qkvrot(cx, l, 3, pp)
        self.sub_agkv(cx, l)
        if at: self.sub_attn(cx, l, 3, it_ix, base)
        if wo: self.sub_wo(cx, l, 3, pp)
        self.agx_export(cx)
        self.sub_agkv_read(cx, l)
        if at: self.sub_attn(cx, l, 0, it_ix, base)
        if wo: self.sub_wo(cx, l, 0, pp)

    def sub_qkvrot(self, cx, l, c, pp):
        col = 1 + CH * c
        woff = l * D * 1536
        qslot = self.qrot[0 if c == 0 else 1]
        kslot = self.kroll[Build.KSLOT[c]]
        vslot = self.vroll[Build.KSLOT[c]]
        evq0 = cx.counts.get("evQ", 0)
        cx.wcur(cx.dve, "sm")
        cx.wcur(cx.sp, "sm")
        for mi, mt in enumerate([0, 1, 2, 3, 8, 9, 10, 11, 4, 5, 6, 7]):
            slot, li = self.wload(cx, self.d_fwqkv.tensor, woff + 128 * mt, 1536, NT, 128)
            cx.wabs(cx.pe, f"dw{slot}", 16 * (li // self.NSLOT + 1))
            if mi >= 2:
                cx.wabs(cx.pe, "evQ", evq0 + mi - 1)
            else:
                cx.wcur(cx.pe, "evQ")
                cx.wcur(cx.pe, "oe")
            pb = pp[mi % 2]
            last = None
            for k in range(NT):
                sh = 1 if k < 4 else 0
                ins = cx.t.matmul(pb[:, :], self.ring[slot][:, 128 * k:128 * k + 128],
                                  self.xb[k][:, col - 1 + sh:col - 1 + sh + CH],
                                  start=(k == 0), stop=(k == NT - 1)) if cx.pe else None
                last = ins if ins is not None else last
            cx.inc("pe", last)
            self.ring_mark(cx, li)
            if mt < 8:
                ft = mt % 4
                cx.wcur(cx.dve, "pe")
                i = cx.v.tensor_copy(self.ust[ft][:, :], pb[:, :]) if cx.dve else None
                cx.inc("evQ", i)
            else:
                vt = mt - 8
                cx.wcur(cx.dve, "pe")
                cx.wabs(cx.dve, "vbD", cx.counts.get("vbD", 0))
                i = cx.v.tensor_copy(self.vfm[:, :], pb[:, :]) if cx.dve else None
                cx.inc("evQ", i)
                cx.wcur(cx.sp, "evQ")
                ins = None
                if cx.sp:
                    dst = self.d_vbf[128 * vt:128 * vt + 128, CH * c:CH * c + CH]
                    ins = cx.q.dma_start(out=dst, in_=self.vfm[:, :])
                cx.inc("vbD", ins, 16)
            if mi == 7:
                self.sub_rot(cx, l, c, "q", qslot)
        self.sub_rot(cx, l, c, "k", kslot)
        cx.wcur(cx.sp, "vbD")
        cx.wcur(cx.sp, "av")
        for tt in range(4):
            ins = None
            if cx.sp:
                ins = cx.q.dma_start_transpose(
                    self.vtm_view(vslot, tt),
                    self.d_vbf[:, CH * c + 128 * tt:CH * c + 128 * tt + 128])
            cx.inc("trD", ins, 16)
        if "och" in self.taps and l == 0:
            cx.wcur(cx.sp, "rnev")
            cx.wcur(cx.sp, "dve")
            cx.wcur(cx.sp, "trD")
            rq = self.rnqT[0 if c == 0 else 1]
            for dst, src in ((self.d_kr_tap, kslot[:, :]), (self.d_vr_tap, vslot[:, :]),
                             (self.d_qr_tap, qslot[:, :])):
                ins = cx.q.dma_start(out=dst[:, 2048 * c:2048 * c + 2048], in_=src) if cx.sp else None
                cx.inc("do", ins, 16)
            ins = cx.q.dma_start(out=self.d_rn_tap[:, 32 * c:32 * c + 32], in_=rq[:, :]) if cx.sp else None
            cx.inc("do", ins, 16)
            cx.wcur(cx.dve, "do")
            cx.wcur(cx.pe, "do")

    def vtm_view(self, vslot, tt):
        return vslot[:, 512 * tt:512 * tt + 512]

    def sub_rot(self, cx, l, c, qk, dstslot):
        tabs = (self.d_qcs, self.d_qss) if qk == "q" else (self.d_kcs, self.d_kss)
        cx.wcur(cx.sp, "dve")
        for i2, tab in enumerate(tabs):
            ins = cx.q.dma_start(out=self.tab[i2][:, :], in_=tab[:, CH * c:CH * c + CH]) if cx.sp else None
            cx.inc("tabD", ins, 16)
        scol = self.c_qs[:, l:l + 1] if qk == "q" else self.c_ks[:, l:l + 1]
        ssw = self.c_qs_sw[:, l:l + 1] if qk == "q" else self.c_ks_sw[:, l:l + 1]
        cx.wcur(cx.dve, "tabD")
        i = cx.v.tensor_scalar_mul(self.tab[0][:, :], self.tab[0][:, :], scol) if cx.dve else None
        cx.inc("dve", i)
        i = cx.v.tensor_scalar_mul(self.tab[1][:, :], self.tab[1][:, :], ssw) if cx.dve else None
        cx.inc("dve", i)
        for ft in range(4):
            cx.wcur(cx.act, "dve")
            cx.wcur(cx.act, "evQ")
            cx.wcur(cx.act, "pe")
            i = cx.s.activation(self.sqb[0][:, :], self.ust[ft][:, :], AF.Square) if cx.act else None
            cx.inc("act", i)
            cx.wcur(cx.pe, "act")
            if ft == 0:
                cx.wcur(cx.pe, "rnev")
            i = cx.t.matmul(self.ps["msc"][0:8, :], self.c_qones[:, 8 * ft:8 * ft + 8],
                            self.sqb[0][:, :], start=(ft == 0), stop=(ft == 3)) if cx.pe else None
            cx.inc("pe", i)
            cx.wcur(cx.sp, "act")
            for blk in range(4):
                sb_ = (32 * ((blk + 1) if blk % 2 == 0 else (blk - 1)))
                ins = cx.q.dma_start(out=self.rh[sb_:sb_ + 32, :],
                                     in_=self.ust[ft][32 * blk:32 * blk + 32, :]) if cx.sp else None
                cx.inc("rh", ins, 16)
            cx.wcur(cx.dve, "tabD")
            cx.wcur(cx.dve, "rh")
            cx.wcur(cx.dve, "act")
            i = cx.v.tensor_mul(self.ust[ft][:, :], self.ust[ft][:, :], self.tab[0][:, :]) if cx.dve else None
            cx.inc("dve", i)
            i = cx.v.tensor_mul(self.rh[:, :], self.rh[:, :], self.tab[1][:, :]) if cx.dve else None
            cx.inc("dve", i)
            if qk == "k":
                i = cx.v.tensor_add(self.ust[ft][:, :], self.ust[ft][:, :], self.rh[:, :]) if cx.dve else None
                cx.inc("dve", i)
            else:
                i = cx.v.tensor_add(dstslot[:, 512 * ft:512 * ft + 512],
                                    self.ust[ft][:, :], self.rh[:, :]) if cx.dve else None
                cx.inc("dve", i)
        cx.wcur(cx.dve, "pe")
        i = cx.v.tensor_scalar_max(self.rnr[0:8, :], self.ps["msc"][0:8, :], 1e-24) if cx.dve else None
        cx.inc("dve", i)
        cx.wcur(cx.act, "dve")
        i = cx.s.activation(self.rnr[0:8, :], self.rnr[0:8, :], AF.Ln) if cx.act else None
        cx.inc("act", i)
        cx.wcur(cx.act, "act")
        i = cx.s.activation(self.rnb[0:8, :], self.rnr[0:8, :], AF.Exp, scale=-0.5) if cx.act else None
        cx.inc("act", i)
        if qk == "q":
            mscbf = self.ps["msc"][:, :].bitcast(BF16)
            cx.wcur(cx.pe, "act")
            cx.wcur(cx.pe, "dve")
            last = None
            for j in range(4):
                i = cx.t.transpose(mscbf[:, 8 * j:8 * j + 8],
                                   self.rnb[0:8, 128 * j:128 * j + 128],
                                   self.c_ident[0:8, 0:8]) if cx.pe else None
                last = i if i is not None else last
                cx.inc("pe", i)
            rq = self.rnqT[0 if c == 0 else 1]
            cx.wcur(cx.dve, "pe")
            i = cx.v.tensor_copy(rq[:, :], mscbf[:, 0:32]) if cx.dve else None
            cx.inc("rnev", i)
        else:
            for ft in range(4):
                cx.wcur(cx.pe, "act")
                cx.wcur(cx.pe, "dve")
                i = cx.t.matmul(self.ps["msc"][:, :], self.c_qsel[:, 128 * ft:128 * ft + 128],
                                self.rnb[0:8, :], start=True, stop=True) if cx.pe else None
                cx.inc("pe", i)
                cx.wcur(cx.dve, "pe")
                i = cx.v.tensor_mul(dstslot[:, 512 * ft:512 * ft + 512],
                                    self.ust[ft][:, :], self.ps["msc"][:, :]) if cx.dve else None
                cx.inc("rnev" if ft == 3 else "dve", i)

    def sub_agkv(self, cx, l):
        cx.wcur(cx.sp, "rnev")
        cx.wcur(cx.sp, "dve")
        cx.wcur(cx.sp, "trD")
        kslot = self.kroll[Build.KSLOT[3]]
        vslot = self.vroll[Build.KSLOT[3]]
        for t in range(4):
            ins = cx.q.dma_start(out=self.d_agkv_in[128 * t:128 * t + 128, :],
                                 in_=kslot[:, 512 * t:512 * t + 512]) if cx.sp else None
            cx.inc("agD", ins, 16)
            ins = cx.q.dma_start(out=self.d_agkv_in[512 + 128 * t:512 + 128 * t + 128, :],
                                 in_=vslot[:, 512 * t:512 * t + 512]) if cx.sp else None
            cx.inc("agD", ins, 16)
        cx.wcur(cx.pool, "agD")
        ins = None
        if cx.pool:
            ins = cx.g.collective_compute(
                "AllGather", OP.bypass, replica_groups=[list(range(NCORES))],
                ins=[self.d_agkv_in[:, :].opt()], outs=[self.d_agkv_out[:, :].opt()])
        cx.inc("cc", ins)

    def sub_agkv_read(self, cx, l):
        cx.wcur(cx.sp, "cc")
        kslot = self.kroll[Build.KSLOT[-1]]
        vslot = self.vroll[Build.KSLOT[-1]]
        for half, dst in ((0, kslot), (1, vslot)):
            ins = None
            if cx.sp:
                rk = self.nc.values_load(self.c_rk[0:1, 0:1], engines=[mybir.EngineType.SP])
                srcap = bass.AP(self.d_agkv_out.tensor, (rk * 1024 + 512 * half) * CH,
                                [[CH, 128], [128 * CH, 4], [1, CH]])
                ins = cx.q.dma_start(out=dst[:, :].rearrange("p (t c) -> p t c", c=CH), in_=srcap)
            cx.inc("agD", ins, 16)

    def sub_attn(self, cx, l, c, it_ix, base):
        cx.wcur(cx.pe, "rnev")
        cx.wcur(cx.pe, "dve")
        cx.wcur(cx.pe, "trD")
        if c == 0:
            cx.wcur(cx.pe, "agD")
        for p in range(4):
            cx.wcur(cx.pe, "oe")
            for j in range(4):
                for h2 in range(2):
                    it = it_ix[0]; it_ix[0] += 1
                    self.attn_iter(cx, l, c, p, j, h2, it, base)
            cx.wcur(cx.dve, "av")
            cx.wcur(cx.dve, "pe")
            i = cx.v.tensor_copy(self.och[:, 512 * p:512 * p + 512], self.ps["o2"][:, :]) if cx.dve else None
            cx.inc("oe", i)
        if "och" in self.taps and l == 0:
            cx.wcur(cx.sp, "oe")
            ins = cx.q.dma_start(out=self.d_och_tap[:, 2048 * c:2048 * c + 2048], in_=self.och[:, :]) if cx.sp else None
            cx.inc("do", ins, 16)
            cx.wcur(cx.dve, "do")

    def attn_iter(self, cx, l, c, p, j, h2, it, base):
        qslot = self.qrot[0 if c == 0 else 1]
        rq = self.rnqT[0 if c == 0 else 1]
        sim = self.ps["sim"]
        cx.wabs(cx.pe, "ex", base["ex"] + it)
        pieces = []
        w1p = 512 - 128 * j
        pieces.append((c - 1, 128 * j, w1p, 0))
        if 128 * j > 0:
            pieces.append((c, 0, 128 * j, w1p))
        pieces.append((c, 128 * j, 128, 512))
        last = None
        for (ck, scol, w, pcol) in pieces:
            kslot = self.kroll[Build.KSLOT[ck]]
            ins = cx.t.matmul(sim[:, pcol:pcol + w],
                              qslot[64 * h2:64 * h2 + 64, 512 * p + 128 * j:512 * p + 128 * j + 128],
                              kslot[64 * h2:64 * h2 + 64, 512 * p + scol:512 * p + scol + w],
                              start=True, stop=True) if cx.pe else None
            last = ins if ins is not None else last
        cx.inc("sm", last)
        cx.wcur(cx.dve, "sm")
        if c == 0:
            Wj = 512 - 128 * j
            off = [0, 512, 896, 1152][j]
            i = cx.v.scalar_tensor_tensor(sim[:, 0:Wj], sim[:, 0:Wj], 1.0,
                                          self.c_mlo[:, off:off + Wj], OP.mult, OP.add) if cx.dve else None
            cx.inc("dve", i)
        else:
            i = cx.v.scalar_tensor_tensor(sim[:, 0:128], sim[:, 0:128], 1.0,
                                          self.c_trilo[:, :], OP.mult, OP.add) if cx.dve else None
            cx.inc("dve", i)
        i = cx.v.scalar_tensor_tensor(sim[:, 512:640], sim[:, 512:640], 1.0,
                                      self.c_trihi[:, :], OP.mult, OP.add) if cx.dve else None
        cx.inc("dve", i)
        i = cx.v.reduce_max(self.negmax[:, :], sim[:, :], AX.X, negate=True) if cx.dve else None
        cx.inc("dve", i)
        cx.wcur(cx.dve, "dve")
        i = cx.v.tensor_mul(self.negmax2[:, :], self.negmax[:, :], rq[:, 8 * j + 2 * p + h2:8 * j + 2 * p + h2 + 1]) if cx.dve else None
        cx.inc("mm", i)
        cx.wabs(cx.act, "tr", base["tr"] + it)
        cx.wabs(cx.act, "sc", base["sc"] + it)
        cx.wcur(cx.act, "mm")
        i = cx.s.activation(self.slab[:, :], sim[:, :], AF.Exp,
                            bias=self.negmax2[:, :], scale=rq[:, 8 * j + 2 * p + h2:8 * j + 2 * p + h2 + 1],
                            accum_out=self.srow[:, :]) if cx.act else None
        cx.inc("ex", i)
        cx.wcur(cx.dve, "ex")
        i = cx.v.reciprocal(self.rrow[:, :], self.srow[:, :]) if cx.dve else None
        cx.inc("dve", i)
        cx.wcur(cx.dve, "dve")
        i = cx.v.tensor_scalar_mul(self.slab[:, :], self.slab[:, :], self.rrow[:, :]) if cx.dve else None
        cx.inc("sc", i)
        cx.wabs(cx.pe, "te", base["te"] + it)
        cx.wcur(cx.pe, "sc")
        last = None
        for c2 in range(5):
            ins = cx.t.transpose(self.ps["aT"][:, 128 * c2:128 * c2 + 128],
                                 self.slab[:, 128 * c2:128 * c2 + 128],
                                 self.c_ident[:, :]) if cx.pe else None
            last = ins if ins is not None else last
        cx.inc("tr", last)
        cx.wabs(cx.act, "av", base["av"] + it)
        cx.wcur(cx.act, "tr")
        i = cx.s.copy(self.aTs[:, :], self.ps["aT"][:, :]) if cx.act else None
        cx.inc("te", i)
        cx.wcur(cx.pe, "te")
        last = None
        for c2 in range(5):
            kt = 512 * c + 128 * j - 512 + 128 * c2
            ck = kt // 512
            tt = (kt % 512) // 128
            vslot = self.vroll[Build.KSLOT[ck]]
            ins = cx.t.matmul(self.ps["o2"][64 * h2:64 * h2 + 64, 128 * j:128 * j + 128],
                              vslot[:, 512 * tt + 64 * (2 * p + h2):512 * tt + 64 * (2 * p + h2) + 64],
                              self.aTs[:, 128 * c2:128 * c2 + 128],
                              start=(c2 == 0), stop=(c2 == 4),
                              tile_position=(0, 64 * h2)) if cx.pe else None
            last = ins if ins is not None else last
        cx.inc("av", last)

    def sub_wo(self, cx, l, c, pp):
        col = 1 + CH * c
        woff = l * 512 * D
        evq0 = cx.counts.get("evQ", 0)
        for ft in range(NT):
            slot, li = self.wload(cx, self.d_fwo.tensor, woff + 128 * ft, D, 4, 128)
            cx.wabs(cx.pe, f"dw{slot}", 16 * (li // self.NSLOT + 1))
            if ft >= 2:
                cx.wabs(cx.pe, "evQ", evq0 + ft - 1)
            cx.wcur(cx.pe, "oe")
            pb = pp[ft % 2]
            last = None
            for k in range(4):
                ins = cx.t.matmul(pb[:, :], self.ring[slot][:, 128 * k:128 * k + 128],
                                  self.och[:, 512 * k:512 * k + 512],
                                  start=(k == 0), stop=(k == 3)) if cx.pe else None
                last = ins if ins is not None else last
            cx.inc("pe", last)
            self.ring_mark(cx, li)
            cx.wcur(cx.dve, "pe")
            i = cx.v.scalar_tensor_tensor(self.xf[ft][:, col:col + CH],
                                          self.xf[ft][:, col:col + CH], 1.0,
                                          pb[:, :], OP.mult, OP.add) if cx.dve else None
            cx.inc("evQ", i)

    def ph_B(self, cx, l):
        psB = [self.ps[f"b{i}"] for i in range(4)]
        psW = [self.ps["w20"], self.ps["w21"]]
        cx.wcur(cx.dve, "act")
        for t in range(NT):
            i = cx.v.tensor_copy(self.bst[t][:, :], self.xf[t][:, 0:1537:512]) if cx.dve else None
            cx.inc("dve", i)
        for c in range(NCH):
            self.sub_ln(cx, l, c, psB)
            self.sub_w1(cx, l, c, psB)
            self.sub_w2(cx, l, c, psW)
        if l == 0:
            self.agx_export(cx)

    def sub_ln(self, cx, l, c, psB):
        col = 1 + CH * c
        for t in range(NT):
            if t >= 4:
                cx.wcur(cx.dve, "lnp")
                i = cx.v.tensor_copy(self.h2f[t - 4][:, :], self.xf[t][:, col - 1:col - 1 + CH]) if cx.dve else None
                cx.inc("dve", i)
                cx.wcur(cx.dve, "dve")
                i = cx.v.tensor_copy(self.h2f[t - 4][:, 0:1], self.bst[t][:, c:c + 1]) if cx.dve else None
                cx.inc("dve", i)
                cx.wcur(cx.act, "dve")
                i = cx.s.copy(self.xpb[t][:, :], self.h2f[t - 4][:, :]) if cx.act else None
                cx.inc("act", i)
            else:
                cx.wcur(cx.act, "lnp")
                cx.wcur(cx.act, "pe")
                i = cx.s.copy(self.xpb[t][:, :], self.xf[t][:, col:col + CH]) if cx.act else None
                cx.inc("act", i)
        for t in range(NT):
            src32 = self.h2f[t - 4] if t >= 4 else None
            cx.wcur(cx.act, "pe")
            if src32 is not None:
                i = cx.s.activation(self.sqb[t % 2][:, :], src32[:, :], AF.Square) if cx.act else None
            else:
                i = cx.s.activation(self.sqb[t % 2][:, :], self.xf[t][:, col:col + CH], AF.Square) if cx.act else None
            cx.inc("act", i)
            cx.wcur(cx.pe, "act")
            if t == 0:
                cx.wcur(cx.pe, "lnp")
                cx.wcur(cx.pe, "hpe")
            i = cx.t.matmul(psB[0][0:1, :], self.c_ones128b[:, :], self.xpb[t][:, :],
                            start=(t == 0), stop=(t == NT - 1)) if cx.pe else None
            cx.inc("pe", i)
            i = cx.t.matmul(psB[1][0:1, :], self.c_ones128b[:, :], self.sqb[t % 2][:, :],
                            start=(t == 0), stop=(t == NT - 1)) if cx.pe else None
            cx.inc("pe", i)
        cx.wcur(cx.dve, "pe")
        i = cx.v.tensor_scalar_mul(self.r_mu[:, :], psB[0][0:1, :], 1.0 / D) if cx.dve else None
        cx.inc("dve", i)
        cx.wcur(cx.act, "dve")
        i = cx.s.activation(self.rowa[:, :], self.r_mu[:, :], AF.Square) if cx.act else None
        cx.inc("act", i)
        cx.wcur(cx.dve, "act")
        i = cx.v.scalar_tensor_tensor(self.rowb[:, :], psB[1][0:1, :], 1.0 / D, self.rowa[:, :],
                                      OP.mult, OP.subtract) if cx.dve else None
        cx.inc("dve", i)
        cx.wcur(cx.act, "dve")
        i = cx.s.activation(self.rowa[:, :], self.rowb[:, :], AF.Ln, bias=1e-5) if cx.act else None
        cx.inc("act", i)
        cx.wcur(cx.act, "act")
        i = cx.s.activation(self.r_rs[:, :], self.rowa[:, :], AF.Exp, scale=-0.5) if cx.act else None
        cx.inc("act", i)
        cx.wcur(cx.pe, "act")
        i = cx.t.matmul(psB[2][:, :], self.c_onesf[:, :], self.r_mu[:, :], start=True, stop=True) if cx.pe else None
        cx.inc("pe", i)
        i = cx.t.matmul(psB[3][:, :], self.c_onesf[:, :], self.r_rs[:, :], start=True, stop=True) if cx.pe else None
        cx.inc("pe", i)
        for t in range(NT):
            src32 = self.h2f[t - 4][:, :] if t >= 4 else self.xf[t][:, col:col + CH]
            cx.wcur(cx.dve, "pe")
            i = cx.v.scalar_tensor_tensor(self.st5[t % 2][:, :], psB[2][:, :], -1.0,
                                          src32, OP.mult, OP.add) if cx.dve else None
            cx.inc("dve", i)
            i = cx.v.tensor_mul(self.st5[t % 2][:, :], self.st5[t % 2][:, :], psB[3][:, :]) if cx.dve else None
            cx.inc("dve", i)
            cx.wcur(cx.act, "dve")
            i = cx.s.activation(self.lnc[t][:, :], self.st5[t % 2][:, :], AF.Identity,
                                scale=self.c_lng[:, 8 * l + t:8 * l + t + 1],
                                bias=self.c_lnb[:, 8 * l + t:8 * l + t + 1]) if cx.act else None
            cx.inc("lnp", i)

    def sub_w1(self, cx, l, c, psB):
        woff1 = l * D * W1W
        hp0 = cx.counts.get("hpe", 0)
        for hp in range(NHP):
            slot, li = self.wload(cx, self.d_fw1.tensor, woff1 + 256 * hp, W1W, NT, 256)
            cx.wabs(cx.pe, f"dw{slot}", 16 * (li // self.NSLOT + 1))
            if hp >= 2:
                cx.wabs(cx.pe, "hpe", hp0 + hp - 1)
            cx.wcur(cx.pe, "lnp")
            pa = psB[2 * (hp % 2)]
            pg = psB[2 * (hp % 2) + 1]
            lasta = lastg = None
            for k in range(NT):
                ins = cx.t.matmul(pa[:, :], self.ring[slot][:, 256 * k:256 * k + 128],
                                  self.lnc[k][:, :], start=(k == 0), stop=(k == NT - 1)) if cx.pe else None
                lasta = ins if ins is not None else lasta
                ins = cx.t.matmul(pg[:, :], self.ring[slot][:, 256 * k + 128:256 * k + 256],
                                  self.lnc[k][:, :], start=(k == 0), stop=(k == NT - 1)) if cx.pe else None
                lastg = ins if ins is not None else lastg
            cx.inc("pe", lastg)
            self.ring_mark(cx, li)
            cx.wcur(cx.act, "pe")
            cx.wcur(cx.act, "dve")
            i = cx.s.activation(self.gl[:, :], pg[:, :], AF.Gelu,
                                bias=self.c_b1[:, 44 * l + 2 * hp + 1:44 * l + 2 * hp + 2]) if cx.act else None
            cx.inc("act", i)
            cx.wcur(cx.dve, "act")
            i = cx.v.scalar_tensor_tensor(self.hid[hp][:, :], pa[:, :],
                                          self.c_b1[:, 44 * l + 2 * hp:44 * l + 2 * hp + 1],
                                          self.gl[:, :], OP.add, OP.mult) if cx.dve else None
            cx.inc("hpe", i)

    def sub_w2(self, cx, l, c, psW):
        col = 1 + CH * c
        woff2 = l * HIDP * D
        w2e0 = cx.counts.get("w2e", 0)
        for ft in range(NT):
            slotA, liA = self.wload(cx, self.d_fw2.tensor, woff2 + 128 * ft, D, 11, 128)
            slotB, liB = self.wload(cx, self.d_fw2.tensor, woff2 + 128 * ft + 1408 * D, D, 11, 128)
            cx.wabs(cx.pe, f"dw{slotA}", 16 * (liA // self.NSLOT + 1))
            cx.wabs(cx.pe, f"dw{slotB}", 16 * (liB // self.NSLOT + 1))
            if ft >= 2:
                cx.wabs(cx.pe, "w2e", w2e0 + ft - 1)
            cx.wcur(cx.pe, "hpe")
            pb = psW[ft % 2]
            last = None
            for hp in range(NHP):
                slot = slotA if hp < 11 else slotB
                kk = hp if hp < 11 else hp - 11
                ins = cx.t.matmul(pb[:, :], self.ring[slot][:, 128 * kk:128 * kk + 128],
                                  self.hid[hp][:, :], start=(hp == 0), stop=(hp == NHP - 1)) if cx.pe else None
                last = ins if ins is not None else last
            cx.inc("pe", last)
            self.ring_mark(cx, liA)
            self.ring_mark(cx, liB)
            cx.wcur(cx.dve, "pe")
            i = cx.v.scalar_tensor_tensor(self.xf[ft][:, col:col + CH],
                                          self.xf[ft][:, col:col + CH],
                                          self.c_b2[:, 8 * l + ft:8 * l + ft + 1],
                                          pb[:, :], OP.add, OP.add) if cx.dve else None
            cx.inc("w2e", i)
            if l == 0:
                cx.wcur(cx.act, "w2e")
                i = cx.s.copy(self.xb[ft][:, col:col + CH], self.xf[ft][:, col:col + CH]) if cx.act else None
                cx.inc("act", i)


def build_program(stages=("p1", "attn", "ffn"), dev_taps=(), ffn_parts=None):
    if ffn_parts is None:
        ffn_parts = [("A", 0), ("B", 0), ("A", 1), ("B", 1)]
    b = Build(dev_taps=dev_taps)
    sb = lambda name, shape, dt: b.es.enter_context(b.nc.sbuf_tensor(name, list(shape), dt))
    b.sqb = [sb(f"sqb{i}", (128, CH), BF16) for i in range(2)]
    b.sqf = [sb(f"sqf{i}", (128, CH), F32) for i in range(2)]
    b.st5 = [sb(f"st5{i}", (128, CH), F32) for i in range(2)]
    b.rowa = sb("rowa", (1, CH), F32)
    b.rowb = sb("rowb", (1, CH), F32)
    b.kred = sb("kred", (128, 1), F32)
    for ch in ("evA", "stD", "trD", "qD", "evZ", "evW", "xD", "ksD", "agD",
               "dw0", "dw1", "dw2", "dw3", "xa0", "xa1", "kvD0", "kvD1"):
        b.sem(ch)

    b.block({}, b.ph_consts)
    if "p1" in stages:
        b.block({"sx": ((1, CH), F32), "sx2": ((1, CH), F32),
                 "mu": ((128, CH), F32), "rs": ((128, CH), F32)}, b.ph_p1)
    with_xacc = "attn" in stages
    if with_xacc:
        with ExitStack() as aes:
            asb = lambda name, shape, dt: aes.enter_context(b.nc.sbuf_tensor(name, list(shape), dt))
            b.ksumc = asb("ksumc", (128, NT), F32)
            b.ksumb = asb("ksumb", (128, NT), BF16)
            b.ksum4 = asb("ksum4", (128, 4 * NT), F32)
            b.ksumt = asb("ksumt", (128, 2 * NT), F32)
            with ExitStack() as a1:
                a1sb = lambda name, shape, dt: a1.enter_context(b.nc.sbuf_tensor(name, list(shape), dt))
                b.qstage = a1sb("qstage", (128, 8 * CH), BF16)
                b.kstage = a1sb("kstage", (128, 8 * CH), BF16)
                b.vstage = a1sb("vstage", (128, 8 * CH), BF16)
                b.block({"qk0": ((128, CH), F32), "qk1": ((128, CH), F32)}, b.ph_p2)
            with ExitStack() as a2:
                a2sb = lambda name, shape, dt: a2.enter_context(b.nc.sbuf_tensor(name, list(shape), dt))
                b.ktm = [a2sb(f"ktm{i}", (128, 512), BF16) for i in range(2)]
                b.vtm = [a2sb(f"vtm{i}", (128, 1024), BF16) for i in range(2)]
                b.kvstage = [a2sb(f"kvstage{i}", (128, 1024), F32) for i in range(2)]
                b.block({f"kv{m}": ((128, 1024), F32) for m in range(4)}, b.ph_p3)
            with ExitStack() as a2b:
                absb = lambda name, shape, dt: a2b.enter_context(b.nc.sbuf_tensor(name, list(shape), dt))
                b.agldt = absb("agldt", (128, NCORES * 1024), F32)
                b.agu = [absb(f"agu{i}", (128, 1024), F32) for i in range(4)]
                b.block({}, b.ph_p3r)
            with ExitStack() as a3:
                a3sb = lambda name, shape, dt: a3.enter_context(b.nc.sbuf_tensor(name, list(shape), dt))
                b.kvb = [a3sb(f"kvb{t}", (128, 1024), BF16) for t in range(NT)]
                b.ksg = a3sb("ksg", (128, NCORES * NT), F32)
                b.ksg2 = a3sb("ksg2", (128, 4 * NT), F32)
                b.qch = [a3sb(f"qch{t}", (128, CH), BF16) for t in range(NT)]
                b.azb = [a3sb(f"azb{t}", (128, CH), BF16) for t in range(NT)]
                b.zbc_sb = a3sb("zbc_sb", (128, CH), F32)
                b.xastage = [a3sb(f"xastage{i}", (128, CH), F32) for i in range(2)]
                if "bounce" in dev_taps:
                    b.ztap_sb = a3sb("ztap_sb", (1, T), F32)
                    b.kstap_sb = a3sb("kstap_sb", (128, NT), F32)
                b.block({"zden": ((1, CH), F32), "zbc": ((128, CH), F32),
                         "at0": ((128, CH), F32), "at1": ((128, CH), F32),
                         "wo0": ((128, CH), F32), "wo1": ((128, CH), F32)}, b.ph_af)
    if "ffn" in stages:
        for ch in ("sm", "mm", "ex", "sc", "tr", "te", "av", "oe", "evQ", "rnev",
                   "rh", "tabD", "vbD", "lnp", "hpe", "w2e"):
            if ch not in b.sems:
                b.sem(ch)
        for l in range(2):
            if ("A", l) in ffn_parts:
              with ExitStack() as fes:
                fsb = lambda name, shape, dt: fes.enter_context(b.nc.sbuf_tensor(f"{name}_l{l}", list(shape), dt))
                b.kroll = [fsb(f"kroll{i}", (128, 2048), BF16) for i in range(4)]
                b.vroll = [fsb(f"vroll{i}", (128, 2048), BF16) for i in range(4)]
                b.qrot = [fsb(f"qrot{i}", (128, 2048), BF16) for i in range(2)]
                b.ust = [fsb(f"ust{i}", (128, CH), F32) for i in range(4)]
                b.rh = fsb("rh", (128, CH), F32)
                b.vfm = fsb("vfm", (128, CH), BF16)
                b.tab = [fsb(f"tab{i}", (128, CH), F32) for i in range(2)]
                b.slab = fsb("slab", (128, 640), BF16)
                b.aTs = fsb("aTs", (128, 640), BF16)
                b.och = fsb("och", (128, 2048), BF16)
                b.rnr = fsb("rnr", (8, CH), F32)
                b.rnb = fsb("rnb", (8, CH), BF16)
                b.rnqT = [fsb(f"rnqT{i}", (128, 32), F32) for i in range(2)]
                b.negmax = fsb("negmax", (128, 1), F32)
                b.negmax2 = fsb("negmax2", (128, 1), F32)
                b.srow = fsb("srow", (128, 1), F32)
                b.rrow = fsb("rrow", (128, 1), F32)
                b.block({"pp0": ((128, CH), F32), "pp1": ((128, CH), F32),
                         "msc": ((128, CH), F32), "sim": ((128, 640), F32),
                         "aT": ((128, 640), BF16), "o2": ((128, CH), F32)},
                        lambda cx, l=l: b.ph_A(cx, l))
            if ("B", l) in ffn_parts:
              with ExitStack() as fes:
                fsb = lambda name, shape, dt: fes.enter_context(b.nc.sbuf_tensor(f"{name}_l{l}", list(shape), dt))
                b.bst = [fsb(f"bst{t}", (128, 4), F32) for t in range(NT)]
                b.h2f = [fsb(f"h2f{t}", (128, CH), F32) for t in range(4)]
                b.xpb = [fsb(f"xpb{t}", (128, CH), BF16) for t in range(NT)]
                b.lnc = [fsb(f"lnc{t}", (128, CH), BF16) for t in range(NT)]
                b.gl = fsb("gl", (128, CH), F32)
                b.hid = [fsb(f"hid{t}", (128, CH), BF16) for t in range(NHP)]
                b.block({"b0": ((128, CH), F32), "b1": ((128, CH), F32),
                         "b2": ((128, CH), F32), "b3": ((128, CH), F32),
                         "w20": ((128, CH), F32), "w21": ((128, CH), F32)},
                        lambda cx, l=l: b.ph_B(cx, l))
    b.block({}, lambda cx: b.ph_finale(cx, with_xacc))
    b.es.close()
    return b


def kernel(**inputs):
    per_core = _host_prep(inputs)
    b = build_program(stages=("p1", "attn", "ffn"))
    res = run_bass_kernel_spmd(b.nc, per_core, core_ids=list(range(NCORES)))
    shards = [np.asarray(r["outt"]) for r in res.results]
    out = np.concatenate([s.T for s in shards], axis=0)[None]
    return out.astype(np.float32)

